# version 16
# speedup vs baseline: 1.1289x; 1.0934x over previous
import numpy as np
import ml_dtypes
from contextlib import ExitStack

import concourse.bass as bass
import concourse.mybir as mybir
from concourse.bass import ds
from concourse.bass_utils import run_bass_kernel_spmd

BFNP = ml_dtypes.bfloat16
F32 = mybir.dt.float32
BF16 = mybir.dt.bfloat16
I32 = mybir.dt.int32
AF = mybir.ActivationFunctionType
ALU = mybir.AluOpType

NCORES = 8
NBLK, BLKSZ, SEQ, BATCH, D = 16, 512, 8192, 2, 1024
TLOC = 4 * BLKSZ
NCH = D // 128
NT = TLOC // 512

ENGINES = ("tensor", "vector", "scalar", "gpsimd", "sync")


def core_blocks(c):
    return (c, NBLK - 1 - c)


class _Op:
    __slots__ = ("idx", "engine", "fn", "reads", "writes", "dma", "tag",
                 "force_inc")

    def __init__(self, idx, engine, fn, reads, writes, dma, tag,
                 force_inc=False):
        self.idx, self.engine, self.fn = idx, engine, fn
        self.reads, self.writes, self.dma, self.tag = reads, writes, dma, tag
        self.force_inc = force_inc


class Prog:
    def __init__(self, name="prog"):
        self.nc = bass.Bass("TRN2", target_bir_lowering=False, debug=False)
        self.stack = ExitStack()
        self._stacks = [self.stack]
        self.ops = []
        self.deps = []
        self._writer = {}
        self._readers = {}
        self._tensors = {}
        self._uid = 0
        self.barriers = []

    def barrier(self):
        self.barriers.append(len(self.ops))
        self._writer = {}
        self._readers = {}

    from contextlib import contextmanager

    @contextmanager
    def phase(self):
        st = ExitStack()
        self._stacks.append(st)
        try:
            yield
        finally:
            self._stacks.pop()
            self.barrier()
            st.close()

    def sbuf(self, name, shape, dtype):
        h = self._stacks[-1].enter_context(
            self.nc.sbuf_tensor(name, list(shape), dtype))
        self._tensors[name] = h
        return h

    def psum(self, name, shape, dtype=F32):
        h = self._stacks[-1].enter_context(
            self.nc.psum_tensor(name, list(shape), dtype))
        self._tensors[name] = h
        return h

    def sbuf_shared(self, name, shape, dtype):
        if name in self._tensors:
            return self._tensors[name]
        h = self._stacks[0].enter_context(
            self.nc.sbuf_tensor(name, list(shape), dtype))
        self._tensors[name] = h
        return h

    def dram_in(self, name, shape, dtype):
        return self.nc.dram_tensor(name, list(shape), dtype, kind="ExternalInput").ap()

    def dram_out(self, name, shape, dtype):
        return self.nc.dram_tensor(name, list(shape), dtype, kind="ExternalOutput").ap()

    def uid(self):
        self._uid += 1
        return self._uid

    def emit(self, engine, fn, reads=(), writes=(), dma=False, tag="",
             force_inc=False):
        idx = len(self.ops)
        op = _Op(idx, engine, fn, tuple(reads), tuple(writes), dma, tag,
                 force_inc)
        dep = set()
        for b in op.reads:
            w = self._writer.get(b)
            if w is not None:
                dep.add(w)
        for b in op.writes:
            w = self._writer.get(b)
            if w is not None:
                dep.add(w)
            for r in self._readers.get(b, ()):
                dep.add(r)
        dep.discard(idx)
        for b in op.writes:
            self._writer[b] = idx
            self._readers[b] = []
        for b in op.reads:
            self._readers.setdefault(b, []).append(idx)
        self.ops.append(op)
        self.deps.append(dep)
        return idx

    def dma(self, queue, out, in_, reads=(), writes=(), tag="", **kw):
        assert queue in ("sync", "gpsimd", "scalar")
        return self.emit(queue, lambda e: e.dma_start(out=out, in_=in_, **kw),
                         reads=reads, writes=writes, dma=True, tag=tag)

    def build(self, output_dma_ops=()):
        ops, deps = self.ops, self.deps
        n = len(ops)
        barrier_set = sorted(set(self.barriers))

        def key_of(op):
            if op.dma:
                assert len(op.writes) >= 1, f"dma op {op.idx} {op.tag} needs dest"
                return ("D", op.writes[0], op.engine)
            return ("E", op.engine)

        pos_of = [0] * n
        counters = {}
        for op in ops:
            k = key_of(op)
            counters[k] = counters.get(k, 0) + 1
            pos_of[op.idx] = counters[k]
        edges = [dict() for _ in range(n)]
        guaranteed = {e: {} for e in ENGINES}
        for op in ops:
            g = guaranteed[op.engine]
            want = {}
            for d in deps[op.idx]:
                dop = ops[d]
                if dop.engine == "tensor" and op.engine == "tensor":
                    continue
                dkey = key_of(dop)
                p = pos_of[d]
                if g.get(dkey, 0) >= p:
                    continue
                if want.get(dkey, 0) < p:
                    want[dkey] = p
            for dkey, p in want.items():
                g[dkey] = max(g.get(dkey, 0), p)
            edges[op.idx] = want
        inc_needed = [False] * n
        by_pos = {}
        for op in ops:
            by_pos[(key_of(op), pos_of[op.idx])] = op.idx
            if op.dma or op.force_inc:
                inc_needed[op.idx] = True
        for i in range(n):
            for dkey, p in edges[i].items():
                if dkey[0] == "E":
                    inc_needed[by_pos[(dkey, p)]] = True
        incpos = {}
        for op in ops:
            k = key_of(op)
            if k[0] == "E" and inc_needed[op.idx]:
                incpos.setdefault(k, []).append(pos_of[op.idx])
        import bisect

        def wait_val(key, p):
            if key[0] == "D":
                return p
            lst = incpos[key]
            j = bisect.bisect_left(lst, p)
            assert j < len(lst), (key, p)
            return j + 1
        for i in range(n):
            edges[i] = {dk: wait_val(dk, p) for dk, p in edges[i].items()}

        nc = self.nc

        bar_sem = None
        bar_info = []
        if barrier_set:
            bar_sem = self.stack.enter_context(nc.semaphore("sbar"))
            cum = {}
            last_comp = {e: None for e in ENGINES}
            bi = 0
            per_op_bar = []
            for op in ops:
                while bi < len(barrier_set) and op.idx >= barrier_set[bi]:
                    bar_info.append((dict(cum), dict(last_comp)))
                    last_comp = {e: None for e in ENGINES}
                    bi += 1
                k = key_of(op)
                cum[k] = cum.get(k, 0) + 1
                if not op.dma:
                    last_comp[op.engine] = op.idx
            while bi < len(barrier_set):
                bar_info.append((dict(cum), dict(last_comp)))
                last_comp = {e: None for e in ENGINES}
                bi += 1
            for _, lc in bar_info:
                for e, i in lc.items():
                    if i is not None:
                        inc_needed[i] = True
            incpos = {}
            for op in ops:
                k = key_of(op)
                if k[0] == "E" and inc_needed[op.idx]:
                    incpos.setdefault(k, []).append(pos_of[op.idx])
            for i in range(n):
                pass
            def wait_val2(key, pp):
                if key[0] == "D":
                    return pp
                lst = incpos[key]
                j = bisect.bisect_left(lst, pp)
                assert j < len(lst), (key, pp)
                return j + 1
            edges2 = [dict() for _ in range(n)]
            guaranteed2 = {e: {} for e in ENGINES}
            gbi = 0
            for op in ops:
                while gbi < len(barrier_set) and op.idx >= barrier_set[gbi]:
                    guaranteed2 = {e: {} for e in ENGINES}
                    gbi += 1
                g = guaranteed2[op.engine]
                want = {}
                for d in deps[op.idx]:
                    dop = ops[d]
                    if dop.engine == "tensor" and op.engine == "tensor":
                        continue
                    dkey = key_of(dop)
                    pp = pos_of[d]
                    if g.get(dkey, 0) >= pp:
                        continue
                    if want.get(dkey, 0) < pp:
                        want[dkey] = pp
                for dkey, pp in want.items():
                    g[dkey] = max(g.get(dkey, 0), pp)
                edges2[op.idx] = {dk: wait_val2(dk, pp) for dk, pp in want.items()}
            edges = edges2

        def seg_of(idx):
            return bisect.bisect_right(barrier_set, idx)
        span = {}
        key_total = {}
        for op in ops:
            if not inc_needed[op.idx]:
                continue
            k = key_of(op)
            s = seg_of(op.idx)
            span[k] = (span[k][0], s) if k in span else (s, s)
            key_total[k] = pos_of[op.idx]
        sems = {}
        sem_base = {}
        slot_pools = {"gpsimd": [], "sync": [], "scalar": []}
        for k in sorted(span.keys(), key=lambda k: (span[k][0], str(k))):
            if k[0] == "E":
                sems[k] = self.stack.enter_context(nc.semaphore(f"se_{k[1]}"))
                sem_base[k] = 0
                continue
            fs, ls = span[k]
            pool = slot_pools[k[2]]
            slot = None
            for sl in pool:
                if sl[2] < fs:
                    slot = sl
                    break
            if slot is None:
                h = self.stack.enter_context(
                    nc.semaphore(f"sd{k[2]}{len(pool)}"))
                slot = [h, 0, -1]
                pool.append(slot)
            sems[k] = slot[0]
            sem_base[k] = slot[1]
            slot[1] += key_total[k]
            slot[2] = ls

        def wait_abs(k, cnt):
            if k[0] == "D":
                return 16 * (sem_base[k] + cnt)
            return cnt
        final_waits = {}
        for i in output_dma_ops:
            k = key_of(ops[i])
            final_waits[k] = max(final_waits.get(k, 0), pos_of[i])

        per_engine = {e: [] for e in ENGINES}
        for op in ops:
            per_engine[op.engine].append(op)

        def cnt_to_inc(key, pos_limit):
            lst = incpos.get(key, [])
            return bisect.bisect_right(lst, pos_limit)

        def run_stream(eng_obj, engine_name):
            ops_here = per_engine[engine_name]
            oi = 0
            for bidx, bpos in enumerate(barrier_set):
                while oi < len(ops_here) and ops_here[oi].idx < bpos:
                    op = ops_here[oi]
                    for dkey, cnt in sorted(edges[op.idx].items(),
                                            key=lambda kv: str(kv[0])):
                        eng_obj.wait_ge(sems[dkey], wait_abs(dkey, cnt))
                    ins = op.fn(eng_obj)
                    if inc_needed[op.idx]:
                        k = key_of(op)
                        ins.then_inc(sems[k], 16 if op.dma else 1)
                    oi += 1
                cum, lastc = bar_info[bidx]
                li = lastc.get(engine_name)
                if li is not None:
                    k = ("E", engine_name)
                    eng_obj.wait_ge(sems[k], cnt_to_inc(k, pos_of[li]))
                if engine_name == "sync":
                    for k, tot in sorted(cum.items(), key=lambda kv: str(kv[0])):
                        if k[0] == "D" and k in sems:
                            eng_obj.wait_ge(sems[k], wait_abs(k, tot))
                eng_obj.sem_inc(bar_sem, 1)
                eng_obj.wait_ge(bar_sem, 5 * (bidx + 1))
            while oi < len(ops_here):
                op = ops_here[oi]
                for dkey, cnt in sorted(edges[op.idx].items(),
                                        key=lambda kv: str(kv[0])):
                    eng_obj.wait_ge(sems[dkey], wait_abs(dkey, cnt))
                ins = op.fn(eng_obj)
                if inc_needed[op.idx]:
                    k = key_of(op)
                    ins.then_inc(sems[k], 16 if op.dma else 1)
                oi += 1
            if engine_name == "sync":
                for dkey, cnt in sorted(final_waits.items(),
                                        key=lambda kv: str(kv[0])):
                    eng_obj.wait_ge(sems[dkey], wait_abs(dkey, cnt))

        with self.nc.Block() as block:
            @block.tensor
            def _(t):
                run_stream(t, "tensor")

            @block.vector
            def _(v):
                run_stream(v, "vector")

            @block.scalar
            def _(s):
                run_stream(s, "scalar")

            @block.gpsimd
            def _(g):
                run_stream(g, "gpsimd")

            @block.sync
            def _(sy):
                run_stream(sy, "sync")

        self.stack.close()
        return self.nc


class Ctx:
    def __init__(self, prog, n_hb=32, n_fa=8, n_ws=4, n_rows=8,
                 with_xn=True, n_pbig=4, with_prow=True, pfx=""):
        self.prog = prog
        self.pfx = pfx
        p = prog
        self.ones_col_f = p.sbuf_shared("ones_col_f", [128, 1], F32)
        self.ones_row_f = p.sbuf_shared("ones_row_f", [1, 128], F32)
        self.ones_col_b = p.sbuf_shared("ones_col_b", [128, 1], BF16)
        self.ones_row_b = p.sbuf_shared("ones_row_b", [1, 128], BF16)
        if not getattr(p, "_ones_init", False):
            p._ones_init = True
            p.emit("vector", lambda e: e.memset(self.ones_col_f[:, :], 1.0),
                   writes=("ones_col_f",))
            p.emit("vector", lambda e: e.memset(self.ones_row_f[:, :], 1.0),
                   writes=("ones_row_f",))
            p.emit("vector", lambda e: e.memset(self.ones_col_b[:, :], 1.0),
                   writes=("ones_col_b",))
            p.emit("vector", lambda e: e.memset(self.ones_row_b[:, :], 1.0),
                   writes=("ones_row_b",))
        self.pbig = []
        for k in range(n_pbig):
            p.psum(f"{pfx}pb{k}", [128, 512], F32)
            self.pbig.append(f"{pfx}pb{k}")
        if with_prow:
            self.prow0 = f"{pfx}prow0"
            self.prow1 = f"{pfx}prow1"
            p.psum(self.prow0, [1, 512], F32)
            p.psum(self.prow1, [1, 512], F32)
        self.hb = [p.sbuf(f"{pfx}hb{k}", [128, 512], BF16) for k in range(n_hb)]
        self.fa = [p.sbuf(f"{pfx}fa{k}", [128, 512], F32) for k in range(n_fa)]
        self.ws = [p.sbuf(f"{pfx}ws{k}", [128, 3072], BF16) for k in range(n_ws)]
        self.rows = [p.sbuf(f"{pfx}row{k}", [1, 512], F32) for k in range(n_rows)]
        if with_xn:
            self.xn = [p.sbuf(f"{pfx}xn{c}", [128, TLOC], BF16)
                       for c in range(NCH)]
            self.xn_names = [t.name for t in self.xn]

    def t(self, name):
        return self.prog._tensors[name]


def emit_ln_tile(ctx, xpfx, nt, out_names, out_slice_full=False, eps=1e-5):
    p = ctx.prog
    s = slice(nt * 512, (nt + 1) * 512)
    oc, orw = ctx.ones_col_f, ctx.ones_row_f
    r_m, r_e2, r_rstd, r_a = [t.name for t in ctx.rows[:4]]
    r_var = r_e2
    for c in range(NCH):
        x = ctx.t(f"{xpfx}{c}")
        p.emit("tensor",
               lambda e, x=x, c=c: e.matmul(
                   ctx.t(ctx.prow0)[:, :], oc[:, :], x[:, s],
                   start=(c == 0), stop=(c == NCH - 1)),
               reads=(f"{xpfx}{c}", "ones_col_f", ctx.prow0), writes=(ctx.prow0,))
    sqb = ctx.hbsq if hasattr(ctx, "hbsq") else None
    if sqb is None:
        t_ = ctx.prog.sbuf(f"{ctx.pfx}sqb0", [128, 512], BF16)
        t2_ = (ctx.prog.sbuf(f"{ctx.pfx}sqb1", [128, 512], BF16)
               if ctx.pfx != "pb" else t_)
        sqb = [t_, t2_]
        ctx.hbsq = sqb
    for c in range(NCH):
        x = ctx.t(f"{xpfx}{c}")
        k = c % 2
        fa = sqb[k]
        fan = fa.name
        p.emit("scalar",
               lambda e, x=x, fa=fa: e.activation(
                   out=fa[:, :], in_=x[:, s], func=AF.Square),
               reads=(f"{xpfx}{c}",), writes=(fan,))
        p.emit("tensor",
               lambda e, fa=fa, c=c: e.matmul(
                   ctx.t(ctx.prow1)[:, :], ctx.ones_col_b[:, :], fa[:, :],
                   start=(c == 0), stop=(c == NCH - 1)),
               reads=(fan, "ones_col_b", ctx.prow1), writes=(ctx.prow1,))
    p.emit("scalar",
           lambda e: e.activation(out=ctx.t(r_m)[:, :], in_=ctx.t(ctx.prow0)[:, :],
                                  func=AF.Copy, scale=1.0 / D),
           reads=(ctx.prow0,), writes=(r_m,))
    p.emit("scalar",
           lambda e: e.activation(out=ctx.t(r_e2)[:, :], in_=ctx.t(ctx.prow1)[:, :],
                                  func=AF.Copy, scale=1.0 / D),
           reads=(ctx.prow1,), writes=(r_e2,))
    p.emit("vector",
           lambda e: e.tensor_tensor(ctx.t(r_rstd)[:, :], ctx.t(r_m)[:, :],
                                     ctx.t(r_m)[:, :], ALU.mult),
           reads=(r_m,), writes=(r_rstd,))
    p.emit("vector",
           lambda e: e.tensor_tensor(ctx.t(r_var)[:, :], ctx.t(r_e2)[:, :],
                                     ctx.t(r_rstd)[:, :], ALU.subtract),
           reads=(r_e2, r_rstd), writes=(r_var,))
    p.emit("vector",
           lambda e: e.tensor_scalar_add(ctx.t(r_var)[:, :], ctx.t(r_var)[:, :],
                                         eps),
           reads=(r_var,), writes=(r_var,))
    p.emit("scalar",
           lambda e: e.activation(out=ctx.t(r_rstd)[:, :], in_=ctx.t(r_var)[:, :],
                                  func=AF.Ln),
           reads=(r_var,), writes=(r_rstd,))
    p.emit("scalar",
           lambda e: e.activation(out=ctx.t(r_rstd)[:, :], in_=ctx.t(r_rstd)[:, :],
                                  func=AF.Exp, scale=-0.5),
           reads=(r_rstd,), writes=(r_rstd,))
    p.emit("vector",
           lambda e: e.tensor_tensor(ctx.t(r_a)[:, :], ctx.t(r_m)[:, :],
                                     ctx.t(r_rstd)[:, :], ALU.mult),
           reads=(r_m, r_rstd), writes=(r_a,))
    pr, pa = ctx.pbig[2], ctx.pbig[3]
    p.emit("tensor",
           lambda e: e.matmul(ctx.t(pr)[:, :], orw[:, :], ctx.t(r_rstd)[:, :],
                              start=True, stop=True),
           reads=(r_rstd, "ones_row_f", pr), writes=(pr,))
    p.emit("tensor",
           lambda e: e.matmul(ctx.t(pa)[:, :], orw[:, :], ctx.t(r_a)[:, :],
                              start=True, stop=True),
           reads=(r_a, "ones_row_f", pa), writes=(pa,))
    for c in range(NCH):
        x = ctx.t(f"{xpfx}{c}")
        k = c % 2
        fa = ctx.fa[6 + k] if len(ctx.fa) > 7 else ctx.fa[k]
        fan = fa.name
        on = out_names[c]
        out_h = ctx.t(on)
        osl = s if out_slice_full else slice(0, 512)
        p.emit("vector",
               lambda e, x=x, fa=fa: e.tensor_tensor(
                   fa[:, :], x[:, s], ctx.t(pr)[:, :], ALU.mult),
               reads=(f"{xpfx}{c}", pr), writes=(fan,))
        p.emit("vector",
               lambda e, out_h=out_h, osl=osl, fa=fa: e.tensor_tensor(
                   out_h[:, osl], fa[:, :], ctx.t(pa)[:, :], ALU.subtract),
               reads=(fan, pa), writes=(on,))


def emit_ln_full(ctx, xpfx, eps=1e-5):
    for nt in range(NT):
        emit_ln_tile(ctx, xpfx, nt, ctx.xn_names,
                     out_slice_full=True, eps=eps)


def emit_l0_ffn(ctx, wg_ap, wu_ap, wd_ap, xpfx="x"):
    p = ctx.prog
    xn = ctx.hb[24:32]
    xn_names = [t.name for t in xn]
    h = ctx.hb[0:24]
    for nt in range(NT):
        s = slice(nt * 512, (nt + 1) * 512)
        emit_ln_tile(ctx, xpfx, nt, xn_names)
        for mh in range(24):
            k = mh % 2
            wgb, wub = ctx.ws[2 * k], ctx.ws[2 * k + 1]
            p.dma("gpsimd", wgb[:, 0:1024], wg_ap[mh], writes=(wgb.name,))
            p.dma("gpsimd", wub[:, 0:1024], wu_ap[mh], writes=(wub.name,))
            pg, pu = ctx.pbig[2 * k], ctx.pbig[2 * k + 1]
            for kc in range(NCH):
                p.emit("tensor",
                       lambda e, kc=kc, wgb=wgb, pg=pg: e.matmul(
                           ctx.t(pg)[:, :], wgb[:, kc * 128:(kc + 1) * 128],
                           xn[kc][:, :], start=(kc == 0), stop=(kc == NCH - 1)),
                       reads=(wgb.name, xn_names[kc], pg), writes=(pg,))
            for kc in range(NCH):
                p.emit("tensor",
                       lambda e, kc=kc, wub=wub, pu=pu: e.matmul(
                           ctx.t(pu)[:, :], wub[:, kc * 128:(kc + 1) * 128],
                           xn[kc][:, :], start=(kc == 0), stop=(kc == NCH - 1)),
                       reads=(wub.name, xn_names[kc], pu), writes=(pu,))
            sg = ctx.fa[k]
            p.emit("scalar",
                   lambda e, sg=sg, pg=pg: e.activation(
                       out=sg[:, :], in_=ctx.t(pg)[:, :], func=AF.Silu,
                       scale=1.0 / 127.0),
                   reads=(pg,), writes=(sg.name,))
            p.emit("vector",
                   lambda e, sg=sg, pu=pu, mh=mh: e.tensor_tensor(
                       h[mh][:, :], sg[:, :], ctx.t(pu)[:, :], ALU.mult),
                   reads=(sg.name, pu), writes=(h[mh].name,))
        for mo in range(NCH):
            k = mo % 2
            wbc = ctx.ws[2 * k]
            p.dma("gpsimd", wbc[:, :], wd_ap[mo], writes=(wbc.name,))
            pd = ctx.pbig[mo % 2]
            for kc in range(24):
                p.emit("tensor",
                       lambda e, kc=kc, wbc=wbc, pd=pd: e.matmul(
                           ctx.t(pd)[:, :], wbc[:, kc * 128:(kc + 1) * 128],
                           h[kc][:, :], start=(kc == 0), stop=(kc == 23)),
                       reads=(wbc.name, h[kc].name, pd), writes=(pd,))
            x = ctx.t(f"{xpfx}{mo}")
            p.emit("vector",
                   lambda e, x=x, s=s, pd=pd: e.scalar_tensor_tensor(
                       x[:, s], ctx.t(pd)[:, :], 1.0 / (127.0 * 127.0), x[:, s],
                       ALU.mult, ALU.add),
                   reads=(pd, f"{xpfx}{mo}"), writes=(f"{xpfx}{mo}",))


def host_shard_x(x):
    xt = np.ascontiguousarray(x.transpose(2, 0, 1))
    shards = []
    for c in range(NCORES):
        bA, bB = core_blocks(c)
        cols = [xt[:, 0, bA * BLKSZ:(bA + 1) * BLKSZ],
                xt[:, 1, bA * BLKSZ:(bA + 1) * BLKSZ],
                xt[:, 0, bB * BLKSZ:(bB + 1) * BLKSZ],
                xt[:, 1, bB * BLKSZ:(bB + 1) * BLKSZ]]
        shards.append(np.ascontiguousarray(np.concatenate(cols, axis=1)))
    return shards


def host_unshard_x(shards):
    xt = np.zeros((D, BATCH, SEQ), dtype=shards[0].dtype)
    for c in range(NCORES):
        bA, bB = core_blocks(c)
        sh = shards[c]
        xt[:, 0, bA * BLKSZ:(bA + 1) * BLKSZ] = sh[:, 0:512]
        xt[:, 1, bA * BLKSZ:(bA + 1) * BLKSZ] = sh[:, 512:1024]
        xt[:, 0, bB * BLKSZ:(bB + 1) * BLKSZ] = sh[:, 1024:1536]
        xt[:, 1, bB * BLKSZ:(bB + 1) * BLKSZ] = sh[:, 1536:2048]
    return np.ascontiguousarray(xt.transpose(1, 2, 0))


def host_tile_lhsT(wT, name_shape=None):
    K, M = wT.shape
    assert K % 128 == 0 and M % 128 == 0
    a = wT.reshape(K // 128, 128, M // 128, 128)
    return np.ascontiguousarray(a.transpose(2, 1, 0, 3))


def build_l0_ffn_launch():
    p = Prog()
    x_ap = p.dram_in("x", [NCH, 128, TLOC], F32)
    wg_ap = p.dram_in("wg", [24, 128, 8, 128], I32)
    wu_ap = p.dram_in("wu", [24, 128, 8, 128], I32)
    wd_ap = p.dram_in("wd", [8, 128, 24, 128], I32)
    xo_ap = p.dram_out("xo", [NCH, 128, TLOC], F32)
    for c in range(NCH):
        p.sbuf(f"x{c}", [128, TLOC], F32)
    ctx = Ctx(p, with_xn=False)
    for c in range(NCH):
        p.dma("gpsimd", ctx.t(f"x{c}")[:, :], x_ap[c], writes=(f"x{c}",))
    emit_l0_ffn(ctx, wg_ap, wu_ap, wd_ap)
    outs = []
    for c in range(NCH):
        outs.append(p.dma("gpsimd", xo_ap[c], ctx.t(f"x{c}")[:, :],
                          reads=(f"x{c}",), writes=(f"xo{c}",)))
    return p.build(output_dma_ops=outs)


def host_prep_l0_ffn(inputs):
    wg = np.asarray(inputs["L0_ffn_w_gate"])
    wu = np.asarray(inputs["L0_ffn_w_up"])
    wd = np.asarray(inputs["L0_ffn_w_down"])
    return {
        "wg": host_tile_lhsT(np.ascontiguousarray(wg.T)),
        "wu": host_tile_lhsT(np.ascontiguousarray(wu.T)),
        "wd": host_tile_lhsT(np.ascontiguousarray(wd.T)),
    }


def MM(ctx, ps, w, x, ps_sl=None, w_sl=None, x_sl=None, start=True, stop=True):
    p = ctx.prog

    def fn(e):
        pt = ctx.t(ps)
        wt = ctx.t(w)
        xt = ctx.t(x)
        return e.matmul(pt[ps_sl] if ps_sl else pt[:, :],
                        wt[w_sl] if w_sl else wt[:, :],
                        xt[x_sl] if x_sl else xt[:, :],
                        start=start, stop=stop)
    return p.emit("tensor", fn, reads=(w, x, ps), writes=(ps,))


def VOP(ctx, opname, out, a, b, out_sl=None, a_sl=None, b_sl=None, engine="vector"):
    p = ctx.prog

    def fn(e):
        ot, at, bt = ctx.t(out), ctx.t(a), ctx.t(b)
        return e.tensor_tensor(ot[out_sl] if out_sl else ot[:, :],
                               at[a_sl] if a_sl else at[:, :],
                               bt[b_sl] if b_sl else bt[:, :],
                               getattr(ALU, opname))
    return p.emit(engine, fn, reads=(a, b), writes=(out,))


def ACT_(ctx, func, out, in_, out_sl=None, in_sl=None, scale=1.0, bias=None,
         bias_sl=None):
    p = ctx.prog
    reads = [in_]
    if isinstance(bias, str):
        reads.append(bias)

    def fn(e):
        ot, it = ctx.t(out), ctx.t(in_)
        kw = dict(scale=scale)
        if isinstance(bias, str):
            bt = ctx.t(bias)
            kw["bias"] = bt[bias_sl] if bias_sl else bt[:, :]
        elif bias is not None:
            kw["bias"] = bias
        return e.activation(out=ot[out_sl] if out_sl else ot[:, :],
                            in_=it[in_sl] if in_sl else it[:, :],
                            func=getattr(AF, func), **kw)
    return p.emit("scalar", fn, reads=tuple(reads), writes=(out,))


def emit_l1_ffn(ctx, aps, xpfx="x"):
    p = ctx.prog
    emit_ln_full(ctx, xpfx)
    sat = p.sbuf("l1sat", [128, 8 * 16], BF16)
    sbt = p.sbuf("l1sbt", [16, 1024], BF16)
    rat = p.sbuf("l1rat", [128, 6 * 8 * 16], BF16)
    rbt = p.sbuf("l1rbt", [16, 6 * 1024], BF16)
    rwt = p.sbuf("l1rwt", [128, 8 * 6], BF16)
    bb = p.sbuf("l1bb", [128, 8], F32)
    rbv = p.sbuf("l1rb", [1, 6], F32)
    sel6 = p.sbuf_shared("sel6_sb", [6, 6 * 16], BF16)
    idn = p.sbuf_shared("ident_sb", [128, 128], BF16)
    p.dma("gpsimd", sat[:, :], aps["sat"], writes=("l1sat",))
    p.dma("gpsimd", sbt[:, :], aps["sbt"], writes=("l1sbt",))
    p.dma("gpsimd", rat[:, :], aps["rat"], writes=("l1rat",))
    p.dma("gpsimd", rbt[:, :], aps["rbt"], writes=("l1rbt",))
    p.dma("gpsimd", rwt[:, :], aps["rwt"], writes=("l1rwt",))
    p.dma("sync", bb[:, :], aps["bb"], writes=("l1bb",))
    p.dma("sync", rbv[:, :], aps["rb"], writes=("l1rb",))
    p.dma("gpsimd", sel6[:, :], aps["sel6"], writes=("sel6_sb",))
    p.dma("gpsimd", idn[:, :], aps["ident"], writes=("ident_sb",))
    rbb = p.sbuf("l1rbb", [128, 6], F32)
    p.emit("tensor", lambda e: e.matmul(ctx.t(ctx.pbig[0])[:, 0:6], ctx.ones_row_f[:, :],
                                        rbv[:, :], start=True, stop=True),
           reads=("l1rb", "ones_row_f", ctx.pbig[0]), writes=(ctx.pbig[0],))
    ACT_(ctx, "Copy", "l1rbb", ctx.pbig[0], in_sl=(slice(None), slice(0, 6)))
    swT = p.sbuf("l1swT", [6, TLOC], BF16)
    lg = p.sbuf("l1lg", [128, 6], F32)
    l2v = p.sbuf("l1l2v", [128, 6], F32)
    e6 = p.sbuf("l1e6", [128, 6], F32)
    sel = p.sbuf("l1sel", [128, 6], F32)
    num = p.sbuf("l1num", [128, 6], F32)
    sw = p.sbuf("l1sw", [128, 6], BF16)
    m1 = p.sbuf("l1m1", [128, 1], F32)
    nm1 = p.sbuf("l1nm1", [128, 1], F32)
    m2 = p.sbuf("l1m2", [128, 1], F32)
    s6 = p.sbuf("l1s6", [128, 1], F32)
    dnr = p.sbuf("l1dnr", [128, 1], F32)
    rec = p.sbuf("l1rec", [128, 1], F32)
    swt_ps = p.psum("pswT", [6, 128], BF16)
    swt_f = p.sbuf("l1swtf", [6, 128], BF16)
    for tt in range(TLOC // 128):
        tsl = slice(tt * 128, (tt + 1) * 128)
        for kc in range(NCH):
            MM(ctx, ctx.pbig[0], ctx.xn_names[kc], "l1rwt",
               ps_sl=(slice(None), slice(0, 6)),
               w_sl=(slice(None), tsl),
               x_sl=(slice(None), slice(kc * 6, (kc + 1) * 6)),
               start=(kc == 0), stop=(kc == NCH - 1))
        VOP(ctx, "add", "l1lg", ctx.pbig[0], "l1rbb",
            a_sl=(slice(None), slice(0, 6)))
        p.emit("vector", lambda e: e.reduce_max(m1[:, :], lg[:, :],
                                                axis=mybir.AxisListType.X),
               reads=("l1lg",), writes=("l1m1",))
        p.emit("vector", lambda e: e.tensor_scalar_mul(nm1[:, :], m1[:, :], -1.0),
               reads=("l1m1",), writes=("l1nm1",))
        p.emit("vector", lambda e: e.tensor_scalar(sel[:, :], lg[:, :], m1[:, :],
                                                   None, ALU.is_equal),
               reads=("l1lg", "l1m1"), writes=("l1sel",))
        p.emit("vector", lambda e: e.scalar_tensor_tensor(
            l2v[:, :], sel[:, :], -1e30, lg[:, :], ALU.mult, ALU.add),
               reads=("l1sel", "l1lg"), writes=("l1l2v",))
        p.emit("vector", lambda e: e.reduce_max(m2[:, :], l2v[:, :],
                                                axis=mybir.AxisListType.X),
               reads=("l1l2v",), writes=("l1m2",))
        ACT_(ctx, "Exp", "l1e6", "l1lg", bias="l1nm1")
        p.emit("vector", lambda e: e.reduce_sum(s6[:, :], e6[:, :],
                                                axis=mybir.AxisListType.X),
               reads=("l1e6",), writes=("l1s6",))
        p.emit("vector", lambda e: e.tensor_scalar(sel[:, :], lg[:, :], m2[:, :],
                                                   None, ALU.is_ge),
               reads=("l1lg", "l1m2"), writes=("l1sel",))
        VOP(ctx, "mult", "l1num", "l1e6", "l1sel")
        p.emit("vector", lambda e: e.reduce_sum(dnr[:, :], num[:, :],
                                                axis=mybir.AxisListType.X),
               reads=("l1num",), writes=("l1dnr",))
        p.emit("vector", lambda e: e.scalar_tensor_tensor(
            dnr[:, :], s6[:, :], 1e-6, dnr[:, :], ALU.mult, ALU.add),
               reads=("l1s6", "l1dnr"), writes=("l1dnr",))
        p.emit("vector", lambda e: e.reciprocal(rec[:, :], dnr[:, :]),
               reads=("l1dnr",), writes=("l1rec",))
        p.emit("vector", lambda e: e.tensor_scalar(sw[:, :], num[:, :], rec[:, :],
                                                   None, ALU.mult),
               reads=("l1num", "l1rec"), writes=("l1sw",))
        p.emit("tensor", lambda e: e.transpose(swt_ps[:, :], sw[:, :], idn[:, :]),
               reads=("l1sw", "ident_sb", "pswT"), writes=("pswT",))
        p.emit("vector", lambda e: e.tensor_copy(swt_f[:, :], swt_ps[:, :]),
               reads=("pswT",), writes=("l1swtf",))
        p.emit("vector", lambda e, tsl=tsl: e.tensor_copy(swT[:, tsl],
                                                          swt_f[:, 0:128]),
               reads=("l1swtf",), writes=("l1swT",))
    for mo in range(NCH):
        k = mo % 2
        wsb = ctx.ws[k]
        p.dma("gpsimd", wsb[:, 0:1024], aps["bw"][mo], writes=(wsb.name,))
        for nt in range(NT):
            s = slice(nt * 512, (nt + 1) * 512)
            pg = ctx.pbig[nt % 2]
            for kc in range(NCH):
                MM(ctx, pg, wsb.name, ctx.xn_names[kc],
                   w_sl=(slice(None), slice(kc * 128, (kc + 1) * 128)),
                   x_sl=(slice(None), s),
                   start=(kc == 0), stop=(kc == NCH - 1))
            x = ctx.t(f"{xpfx}{mo}")
            p.emit("vector",
                   lambda e, x=x, s=s, pg=pg, mo=mo: e.scalar_tensor_tensor(
                       x[:, s], ctx.t(pg)[:, :], bb[:, mo:mo + 1], x[:, s],
                       ALU.add, ALU.add),
                   reads=(pg, f"{xpfx}{mo}", "l1bb"), writes=(f"{xpfx}{mo}",))
    ps16 = p.psum("ps16", [16, 512], F32)
    u1 = p.sbuf("l1u1", [16, 512], BF16)
    he = [p.sbuf(f"l1he{e}", [16, 512], BF16) for e in range(6)]
    hw = [p.sbuf(f"l1hw{e}", [16, 512], BF16) for e in range(6)]
    for nt in range(NT):
        s = slice(nt * 512, (nt + 1) * 512)
        for kc in range(NCH):
            MM(ctx, "ps16", "l1sat", ctx.xn_names[kc],
               w_sl=(slice(None), slice(kc * 16, (kc + 1) * 16)),
               x_sl=(slice(None), s),
               start=(kc == 0), stop=(kc == NCH - 1))
        ACT_(ctx, "Copy", "l1u1", "ps16")
        for e_ in range(6):
            for kc in range(NCH):
                MM(ctx, "ps16", "l1rat", ctx.xn_names[kc],
                   w_sl=(slice(None), slice((e_ * 8 + kc) * 16,
                                            (e_ * 8 + kc + 1) * 16)),
                   x_sl=(slice(None), s),
                   start=(kc == 0), stop=(kc == NCH - 1))
            ACT_(ctx, "Copy", f"l1he{e_}", "ps16")
            p.emit("tensor",
                   lambda e, e_=e_, s=s: e.matmul(
                       ps16[:, :], sel6[:, e_ * 16:(e_ + 1) * 16], swT[:, s],
                       start=True, stop=True),
                   reads=("sel6_sb", "l1swT", "ps16"), writes=("ps16",))
            VOP(ctx, "mult", f"l1hw{e_}", f"l1he{e_}", "ps16")
        for mo in range(NCH):
            pg = ctx.pbig[2 + (mo % 2)]
            MM(ctx, pg, "l1sbt", "l1u1",
               w_sl=(slice(None), slice(mo * 128, (mo + 1) * 128)),
               start=True, stop=False)
            for e_ in range(6):
                MM(ctx, pg, "l1rbt", f"l1hw{e_}",
                   w_sl=(slice(None), slice(e_ * 1024 + mo * 128,
                                            e_ * 1024 + (mo + 1) * 128)),
                   start=False, stop=(e_ == 5))
            x = ctx.t(f"{xpfx}{mo}")
            p.emit("vector",
                   lambda e, x=x, s=s, pg=pg: e.scalar_tensor_tensor(
                       x[:, s], ctx.t(pg)[:, :], 1.0, x[:, s],
                       ALU.mult, ALU.add),
                   reads=(pg, f"{xpfx}{mo}"), writes=(f"{xpfx}{mo}",))


def host_prep_l1_ffn(inputs):
    g = lambda k: np.ascontiguousarray(np.asarray(inputs[k]))
    baseW = g("L1_ffn_base_W")
    sA = g("L1_ffn_shared_A")
    sB = g("L1_ffn_shared_B")
    rA = g("L1_ffn_routed_A")
    rB = g("L1_ffn_routed_B")
    rW = g("L1_ffn_router_W")
    sel6 = np.zeros((6, 6 * 16), dtype=BFNP)
    for e in range(6):
        sel6[e, e * 16:(e + 1) * 16] = 1.0
    ident = np.eye(128, dtype=BFNP)
    return {
        "bw": host_tile_lhsT(np.ascontiguousarray(baseW.T)),
        "sat": np.ascontiguousarray(sA.T.reshape(8, 128, 16).transpose(1, 0, 2)),
        "sbt": np.ascontiguousarray(sB.T),
        "rat": np.ascontiguousarray(
            rA.transpose(2, 0, 1).reshape(8, 128, 6, 16).transpose(1, 2, 0, 3)),
        "rbt": np.ascontiguousarray(
            rB.transpose(2, 0, 1).reshape(16, 6 * 1024)),
        "rwt": np.ascontiguousarray(rW.T.reshape(8, 128, 6).transpose(1, 0, 2)),
        "bb": np.ascontiguousarray(g("L1_ffn_base_b").reshape(8, 128).T),
        "rb": np.ascontiguousarray(g("L1_ffn_router_b").reshape(1, 6)),
        "sel6": sel6,
        "ident": ident,
    }


def build_l1_ffn_launch():
    p = Prog()
    x_ap = p.dram_in("x", [NCH, 128, TLOC], F32)
    aps = {
        "bw": p.dram_in("bw", [8, 128, 8, 128], F32),
        "sat": p.dram_in("sat", [128, 8, 16], F32),
        "sbt": p.dram_in("sbt", [16, 1024], F32),
        "rat": p.dram_in("rat", [128, 6, 8, 16], F32),
        "rbt": p.dram_in("rbt", [16, 6 * 1024], F32),
        "rwt": p.dram_in("rwt", [128, 8, 6], F32),
        "bb": p.dram_in("bb", [128, 8], F32),
        "rb": p.dram_in("rb", [1, 6], F32),
        "sel6": p.dram_in("sel6", [6, 6 * 16], BF16),
        "ident": p.dram_in("ident", [128, 128], BF16),
    }
    xo_ap = p.dram_out("xo", [NCH, 128, TLOC], F32)
    for c in range(NCH):
        p.sbuf(f"x{c}", [128, TLOC], F32)
    ctx = Ctx(p, n_hb=8, n_fa=4, n_ws=2)
    for c in range(NCH):
        p.dma("gpsimd", ctx.t(f"x{c}")[:, :], x_ap[c], writes=(f"x{c}",))
    emit_l1_ffn(ctx, aps)
    outs = []
    for c in range(NCH):
        outs.append(p.dma("gpsimd", xo_ap[c], ctx.t(f"x{c}")[:, :],
                          reads=(f"x{c}",), writes=(f"xo{c}",)))
    return p.build(output_dma_ops=outs)


def emit_topk_gate(ctx, pfx, wgate_name, nE, wT_name, aps_ident="ident_sb",
                   renorm=False, xnpfx="xn"):
    p = ctx.prog
    u = p.uid()
    lg = p.sbuf(f"{pfx}lg", [128, nE], F32)
    l2v = p.sbuf(f"{pfx}l2v", [128, nE], F32)
    sel = p.sbuf(f"{pfx}sel", [128, nE], F32)
    eab = p.sbuf(f"{pfx}eab", [128, nE], F32)
    wE = p.sbuf(f"{pfx}wE", [128, nE], BF16)
    m1 = p.sbuf(f"{pfx}m1", [128, 1], F32)
    nm1 = p.sbuf(f"{pfx}nm1", [128, 1], F32)
    m2 = p.sbuf(f"{pfx}m2", [128, 1], F32)
    d2 = p.sbuf(f"{pfx}d2", [128, 1], F32)
    rec = p.sbuf(f"{pfx}rec", [128, 1], F32)
    wTps = p.psum(f"{pfx}wTps", [nE, 128], BF16)
    wT = p.sbuf(wT_name, [nE, TLOC], BF16)
    for tt in range(TLOC // 128):
        tsl = slice(tt * 128, (tt + 1) * 128)
        for kc in range(NCH):
            MM(ctx, ctx.pbig[0], ctx.xn_names[kc], wgate_name,
               ps_sl=(slice(None), slice(0, nE)),
               w_sl=(slice(None), tsl),
               x_sl=(slice(None), slice(kc * nE, (kc + 1) * nE)),
               start=(kc == 0), stop=(kc == NCH - 1))
        p.emit("vector", lambda e: e.tensor_copy(lg[:, :],
                                                 ctx.t(ctx.pbig[0])[:, 0:nE]),
               reads=(ctx.pbig[0],), writes=(f"{pfx}lg",))
        p.emit("vector", lambda e: e.reduce_max(m1[:, :], lg[:, :],
                                                axis=mybir.AxisListType.X),
               reads=(f"{pfx}lg",), writes=(f"{pfx}m1",))
        p.emit("vector", lambda e: e.tensor_scalar_mul(nm1[:, :], m1[:, :], -1.0),
               reads=(f"{pfx}m1",), writes=(f"{pfx}nm1",))
        p.emit("vector", lambda e: e.tensor_scalar(sel[:, :], lg[:, :], m1[:, :],
                                                   None, ALU.is_equal),
               reads=(f"{pfx}lg", f"{pfx}m1"), writes=(f"{pfx}sel",))
        p.emit("vector", lambda e: e.scalar_tensor_tensor(
            l2v[:, :], sel[:, :], -1e30, lg[:, :], ALU.mult, ALU.add),
               reads=(f"{pfx}sel", f"{pfx}lg"), writes=(f"{pfx}l2v",))
        p.emit("vector", lambda e: e.reduce_max(m2[:, :], l2v[:, :],
                                                axis=mybir.AxisListType.X),
               reads=(f"{pfx}l2v",), writes=(f"{pfx}m2",))
        ACT_(ctx, "Exp", f"{pfx}eab", f"{pfx}lg", bias=f"{pfx}nm1")
        ACT_(ctx, "Exp", f"{pfx}d2", f"{pfx}m2", bias=f"{pfx}nm1")
        p.emit("vector", lambda e: e.tensor_scalar_add(d2[:, :], d2[:, :], 1.0),
               reads=(f"{pfx}d2",), writes=(f"{pfx}d2",))
        p.emit("vector", lambda e: e.reciprocal(rec[:, :], d2[:, :]),
               reads=(f"{pfx}d2",), writes=(f"{pfx}rec",))
        p.emit("vector", lambda e: e.tensor_scalar(sel[:, :], lg[:, :], m2[:, :],
                                                   None, ALU.is_ge),
               reads=(f"{pfx}lg", f"{pfx}m2"), writes=(f"{pfx}sel",))
        p.emit("vector", lambda e: e.tensor_tensor(eab[:, :], eab[:, :], sel[:, :],
                                                   ALU.mult),
               reads=(f"{pfx}eab", f"{pfx}sel"), writes=(f"{pfx}eab",))
        p.emit("vector", lambda e: e.tensor_scalar(wE[:, :], eab[:, :], rec[:, :],
                                                   None, ALU.mult),
               reads=(f"{pfx}eab", f"{pfx}rec"), writes=(f"{pfx}wE",))
        p.emit("tensor", lambda e: e.transpose(wTps[:, :], wE[:, :],
                                               ctx.t(aps_ident)[:, :]),
               reads=(f"{pfx}wE", aps_ident, f"{pfx}wTps"), writes=(f"{pfx}wTps",))
        p.emit("vector", lambda e, tsl=tsl: e.tensor_copy(wT[:, tsl], wTps[:, :]),
               reads=(f"{pfx}wTps",), writes=(wT_name,))


def emit_l2_ffn(ctx, aps, xpfx="x"):
    p = ctx.prog
    emit_ln_full(ctx, xpfx)
    gw = p.sbuf("l2gw", [128, 8 * 8], BF16)
    b1 = p.sbuf("l2b1", [128, 8 * 16], F32)
    b2 = p.sbuf("l2b2", [128, 8 * 8], F32)
    lng = p.sbuf("l2lng", [128, 8 * 8], F32)
    lnb = p.sbuf("l2lnb", [128, 8 * 8], F32)
    sel8 = p.sbuf_shared("sel8_sb", [8, 8 * 128], BF16)
    idn = p.sbuf_shared("ident_sb", [128, 128], BF16)
    p.dma("gpsimd", gw[:, :], aps["gw"], writes=("l2gw",))
    p.dma("sync", b1[:, :], aps["b1"], writes=("l2b1",))
    p.dma("sync", b2[:, :], aps["b2"], writes=("l2b2",))
    p.dma("sync", lng[:, :], aps["lng"], writes=("l2lng",))
    p.dma("sync", lnb[:, :], aps["lnb"], writes=("l2lnb",))
    p.dma("gpsimd", sel8[:, :], aps["sel8"], writes=("sel8_sb",))
    p.dma("gpsimd", idn[:, :], aps["ident"], writes=("ident_sb",))
    emit_topk_gate(ctx, "l2g", "l2gw", 8, "l2wT")
    h = ctx.hb[0:32]
    o = ctx.hb[32:48]
    wEb = p.sbuf("l2wEb", [128, 512], BF16)
    r_m, r_e2, r_var, r_rstd, r_a = [t.name for t in ctx.rows[:5]]
    for e_ in range(8):
        for np_ in range(NT // 2):
            ss = [slice((2 * np_ + j) * 512, (2 * np_ + j + 1) * 512)
                  for j in range(2)]
            for m in range(16):
                k = m % 2
                wsb = ctx.ws[k]
                p.dma("gpsimd", wsb[:, 0:1024], aps["w1"][e_, m],
                      writes=(wsb.name,))
                for kc in range(NCH):
                    for j in range(2):
                        MM(ctx, ctx.pbig[j], wsb.name, ctx.xn_names[kc],
                           w_sl=(slice(None), slice(kc * 128, (kc + 1) * 128)),
                           x_sl=(slice(None), ss[j]),
                           start=(kc == 0), stop=(kc == NCH - 1))
                for j in range(2):
                    ACT_(ctx, "Gelu", h[2 * m + j].name, ctx.pbig[j],
                         bias="l2b1",
                         bias_sl=(slice(None), slice(e_ * 16 + m,
                                                     e_ * 16 + m + 1)))
            for mo in range(NCH):
                k = mo % 2
                wsb = ctx.ws[k]
                p.dma("gpsimd", wsb[:, 0:2048], aps["w2"][e_, mo],
                      writes=(wsb.name,))
                for kc in range(16):
                    for j in range(2):
                        MM(ctx, ctx.pbig[j], wsb.name, h[2 * kc + j].name,
                           w_sl=(slice(None), slice(kc * 128, (kc + 1) * 128)),
                           start=(kc == 0), stop=(kc == 15))
                for j in range(2):
                    ACT_(ctx, "Identity", o[2 * mo + j].name, ctx.pbig[j],
                         bias="l2b2",
                         bias_sl=(slice(None), slice(e_ * 8 + mo,
                                                     e_ * 8 + mo + 1)))
            for j in range(2):
                s = ss[j]
                for c in range(NCH):
                    xnc = ctx.xn[c]
                    v = ctx.fa[c]
                    p.emit("vector",
                           lambda e, xnc=xnc, v=v, c=c, s=s, j=j: e.tensor_tensor(
                               v[:, :], xnc[:, s], o[2 * c + j][:, :], ALU.add),
                           reads=(ctx.xn_names[c], o[2 * c + j].name),
                           writes=(v.name,))
                    MM(ctx, ctx.prow0, "ones_col_f", v.name,
                       start=(c == 0), stop=(c == NCH - 1))
                for c in range(NCH):
                    v = ctx.fa[c]
                    ACT_(ctx, "Square", "l2sqb", v.name)
                    MM(ctx, ctx.prow1, "ones_col_b", "l2sqb",
                       start=(c == 0), stop=(c == NCH - 1))
                ACT_(ctx, "Copy", r_m, ctx.prow0, scale=1.0 / D)
                ACT_(ctx, "Copy", r_e2, ctx.prow1, scale=1.0 / D)
                VOP(ctx, "mult", r_var, r_m, r_m)
                VOP(ctx, "subtract", r_var, r_e2, r_var)
                p.emit("vector", lambda e: e.tensor_scalar_add(
                    ctx.t(r_var)[:, :], ctx.t(r_var)[:, :], 1e-6),
                       reads=(r_var,), writes=(r_var,))
                ACT_(ctx, "Ln", r_rstd, r_var)
                ACT_(ctx, "Exp", r_rstd, r_rstd, scale=-0.5)
                VOP(ctx, "mult", r_a, r_m, r_rstd)
                pr, pa = ctx.pbig[2], ctx.pbig[3]
                MM(ctx, pr, "ones_row_f", r_rstd)
                MM(ctx, pa, "ones_row_f", r_a)
                p.emit("tensor",
                       lambda e, e_=e_, s=s: e.matmul(
                           ctx.t(ctx.pbig[0])[:, :],
                           sel8[:, e_ * 128:(e_ + 1) * 128],
                           ctx.t("l2wT")[:, s], start=True, stop=True),
                       reads=("sel8_sb", "l2wT", ctx.pbig[0]),
                       writes=(ctx.pbig[0],))
                ACT_(ctx, "Copy", "l2wEb", ctx.pbig[0])
                for c in range(NCH):
                    v = ctx.fa[c]
                    t1 = p._tensors["l2t1"]
                    x = ctx.t(f"{xpfx}{c}")
                    p.emit("vector",
                           lambda e, v=v: e.tensor_tensor(
                               t1[:, :], v[:, :], ctx.t(pr)[:, :], ALU.mult),
                           reads=(v.name, pr), writes=("l2t1",))
                    p.emit("vector",
                           lambda e: e.tensor_tensor(
                               t1[:, :], t1[:, :], ctx.t(pa)[:, :],
                               ALU.subtract),
                           reads=("l2t1", pa), writes=("l2t1",))
                    p.emit("vector",
                           lambda e, c=c, e_=e_: e.tensor_scalar(
                               t1[:, :], t1[:, :],
                               lng[:, e_ * 8 + c:e_ * 8 + c + 1],
                               lnb[:, e_ * 8 + c:e_ * 8 + c + 1],
                               ALU.mult, ALU.add),
                           reads=("l2t1", "l2lng", "l2lnb"), writes=("l2t1",))
                    p.emit("vector",
                           lambda e: e.tensor_tensor(t1[:, :], t1[:, :],
                                                     wEb[:, :], ALU.mult),
                           reads=("l2t1", "l2wEb"), writes=("l2t1",))
                    p.emit("vector",
                           lambda e, x=x, s=s: e.tensor_tensor(
                               x[:, s], x[:, s], t1[:, :], ALU.add),
                           reads=(f"{xpfx}{c}", "l2t1"), writes=(f"{xpfx}{c}",))


def host_prep_l2_ffn(inputs):
    g = lambda k: np.ascontiguousarray(np.asarray(k if isinstance(k, np.ndarray) else inputs[k]))
    gw = g("L2_ffn_gate_w")
    w1 = g("L2_ffn_w1")
    w2 = g("L2_ffn_w2")
    sel8 = np.zeros((8, 8 * 128), dtype=BFNP)
    for e in range(8):
        sel8[e, e * 128:(e + 1) * 128] = 1.0
    w1t = np.stack([host_tile_lhsT(w1[e]) for e in range(8)])
    w2t = np.stack([host_tile_lhsT(w2[e]) for e in range(8)])
    return {
        "gw": np.ascontiguousarray(gw.reshape(8, 128, 8).transpose(1, 0, 2)),
        "w1": w1t, "w2": w2t,
        "b1": np.ascontiguousarray(
            g("L2_ffn_b1").reshape(8, 16, 128).transpose(2, 0, 1)),
        "b2": np.ascontiguousarray(
            g("L2_ffn_b2").reshape(8, 8, 128).transpose(2, 0, 1)),
        "lng": np.ascontiguousarray(
            g("L2_ffn_ln_g").reshape(8, 8, 128).transpose(2, 0, 1)),
        "lnb": np.ascontiguousarray(
            g("L2_ffn_ln_b").reshape(8, 8, 128).transpose(2, 0, 1)),
        "sel8": sel8,
        "ident": np.eye(128, dtype=BFNP),
    }


def build_l2_ffn_launch():
    p = Prog()
    x_ap = p.dram_in("x", [NCH, 128, TLOC], F32)
    aps = {
        "gw": p.dram_in("gw", [128, 8, 8], F32),
        "w1": p.dram_in("w1", [8, 16, 128, 8, 128], F32),
        "w2": p.dram_in("w2", [8, 8, 128, 16, 128], F32),
        "b1": p.dram_in("b1", [128, 8, 16], F32),
        "b2": p.dram_in("b2", [128, 8, 8], F32),
        "lng": p.dram_in("lng", [128, 8, 8], F32),
        "lnb": p.dram_in("lnb", [128, 8, 8], F32),
        "sel8": p.dram_in("sel8", [8, 8 * 128], BF16),
        "ident": p.dram_in("ident", [128, 128], BF16),
    }
    xo_ap = p.dram_out("xo", [NCH, 128, TLOC], F32)
    for c in range(NCH):
        p.sbuf(f"x{c}", [128, TLOC], F32)
    ctx = Ctx(p, n_hb=48, n_fa=8, n_ws=2)
    p.sbuf("l2sq", [128, 512], F32)
    p.sbuf("l2sqb", [128, 512], BF16)
    p.sbuf("l2t1", [128, 512], F32)
    for c in range(NCH):
        p.dma("gpsimd", ctx.t(f"x{c}")[:, :], x_ap[c], writes=(f"x{c}",))
    emit_l2_ffn(ctx, aps)
    outs = []
    for c in range(NCH):
        outs.append(p.dma("gpsimd", xo_ap[c], ctx.t(f"x{c}")[:, :],
                          reads=(f"x{c}",), writes=(f"xo{c}",)))
    return p.build(output_dma_ops=outs)


def emit_phi_gate(ctx, pfx, wgate_name, nE, wT_name, jitter=0.01):
    p = ctx.prog
    lg = p.sbuf(f"{pfx}lg", [128, nE], F32)
    al = p.sbuf(f"{pfx}al", [128, nE], F32)
    f1 = p.sbuf(f"{pfx}f1", [128, nE], F32)
    kp = p.sbuf(f"{pfx}kp", [128, nE], F32)
    eqm1 = p.sbuf(f"{pfx}eqm1", [128, nE], F32)
    eqm2 = p.sbuf(f"{pfx}eqm2", [128, nE], F32)
    l2v = p.sbuf(f"{pfx}l2v", [128, nE], F32)
    ex = p.sbuf(f"{pfx}ex", [128, nE], F32)
    wte = p.sbuf(f"{pfx}wte", [128, nE], BF16)
    m1 = p.sbuf(f"{pfx}m1", [128, 1], F32)
    nm1 = p.sbuf(f"{pfx}nm1", [128, 1], F32)
    m2 = p.sbuf(f"{pfx}m2", [128, 1], F32)
    nm2 = p.sbuf(f"{pfx}nm2", [128, 1], F32)
    z = p.sbuf(f"{pfx}z", [128, 1], F32)
    mu1 = p.sbuf(f"{pfx}mu1", [128, 1], F32)
    mu2 = p.sbuf(f"{pfx}mu2", [128, 1], F32)
    wTps = p.psum(f"{pfx}wTps", [nE, 128], BF16)
    wT = p.sbuf(wT_name, [nE, TLOC], BF16)
    X = mybir.AxisListType.X
    for tt in range(TLOC // 128):
        tsl = slice(tt * 128, (tt + 1) * 128)
        for kc in range(NCH):
            MM(ctx, ctx.pbig[0], ctx.xn_names[kc], wgate_name,
               ps_sl=(slice(None), slice(0, nE)),
               w_sl=(slice(None), tsl),
               x_sl=(slice(None), slice(kc * nE, (kc + 1) * nE)),
               start=(kc == 0), stop=(kc == NCH - 1))
        p.emit("vector", lambda e: e.tensor_copy(lg[:, :], ctx.t(ctx.pbig[0])[:, 0:nE]),
               reads=(ctx.pbig[0],), writes=(lg.name,))
        p.emit("vector", lambda e: e.reduce_max(m1[:, :], lg[:, :], axis=X),
               reads=(lg.name,), writes=(m1.name,))
        p.emit("vector", lambda e: e.tensor_scalar_mul(nm1[:, :], m1[:, :], -1.0),
               reads=(m1.name,), writes=(nm1.name,))
        p.emit("vector", lambda e: e.tensor_scalar(eqm1[:, :], lg[:, :], m1[:, :],
                                                   None, ALU.is_equal),
               reads=(lg.name, m1.name), writes=(eqm1.name,))
        p.emit("vector", lambda e: e.scalar_tensor_tensor(
            l2v[:, :], eqm1[:, :], -1e30, lg[:, :], ALU.mult, ALU.add),
               reads=(eqm1.name, lg.name), writes=(l2v.name,))
        p.emit("vector", lambda e: e.reduce_max(m2[:, :], l2v[:, :], axis=X),
               reads=(l2v.name,), writes=(m2.name,))
        p.emit("vector", lambda e: e.tensor_scalar_mul(nm2[:, :], m2[:, :], -1.0),
               reads=(m2.name,), writes=(nm2.name,))
        p.emit("vector", lambda e: e.tensor_scalar(eqm2[:, :], l2v[:, :], m2[:, :],
                                                   None, ALU.is_equal),
               reads=(l2v.name, m2.name), writes=(eqm2.name,))
        p.emit("scalar", lambda e: e.activation(out=al[:, :], in_=lg[:, :],
                                                func=AF.Abs),
               reads=(lg.name,), writes=(al.name,))
        p.emit("vector", lambda e: e.tensor_scalar(f1[:, :], al[:, :], m1[:, :],
                                                   None, ALU.max),
               reads=(al.name, m1.name), writes=(f1.name,))
        p.emit("vector", lambda e: e.tensor_scalar(kp[:, :], lg[:, :], m1[:, :],
                                                   None, ALU.subtract),
               reads=(lg.name, m1.name), writes=(kp.name,))
        p.emit("vector", lambda e: e.scalar_tensor_tensor(
            kp[:, :], f1[:, :], 2.0 * 0.01, kp[:, :], ALU.mult, ALU.add),
               reads=(f1.name, kp.name), writes=(kp.name,))
        p.emit("vector", lambda e: e.tensor_scalar(kp[:, :], kp[:, :], 0.0,
                                                   None, ALU.is_ge),
               reads=(kp.name,), writes=(kp.name,))
        ACT_(ctx, "Exp", ex.name, lg.name, bias=nm1.name)
        p.emit("vector", lambda e: e.tensor_tensor(ex[:, :], ex[:, :], kp[:, :],
                                                   ALU.mult),
               reads=(ex.name, kp.name), writes=(ex.name,))
        p.emit("vector", lambda e: e.reduce_sum(z[:, :], ex[:, :], axis=X),
               reads=(ex.name,), writes=(z.name,))
        p.emit("vector", lambda e: e.reciprocal(mu1[:, :], z[:, :]),
               reads=(z.name,), writes=(mu1.name,))
        p.emit("vector", lambda e: e.tensor_scalar(f1[:, :], al[:, :], m2[:, :],
                                                   None, ALU.max),
               reads=(al.name, m2.name), writes=(f1.name,))
        p.emit("vector", lambda e: e.tensor_scalar(kp[:, :], lg[:, :], m2[:, :],
                                                   None, ALU.subtract),
               reads=(lg.name, m2.name), writes=(kp.name,))
        p.emit("vector", lambda e: e.scalar_tensor_tensor(
            kp[:, :], f1[:, :], 2.0 * 0.01, kp[:, :], ALU.mult, ALU.add),
               reads=(f1.name, kp.name), writes=(kp.name,))
        p.emit("vector", lambda e: e.tensor_scalar(kp[:, :], kp[:, :], 0.0,
                                                   None, ALU.is_ge),
               reads=(kp.name,), writes=(kp.name,))
        p.emit("vector", lambda e: e.tensor_scalar(eqm1[:, :], eqm1[:, :], -1.0,
                                                   1.0, ALU.mult, ALU.add),
               reads=(eqm1.name,), writes=(eqm1.name,))
        p.emit("vector", lambda e: e.tensor_tensor(kp[:, :], kp[:, :], eqm1[:, :],
                                                   ALU.mult),
               reads=(kp.name, eqm1.name), writes=(kp.name,))
        ACT_(ctx, "Exp", ex.name, lg.name, bias=nm2.name)
        p.emit("vector", lambda e: e.tensor_tensor(ex[:, :], ex[:, :], kp[:, :],
                                                   ALU.mult),
               reads=(ex.name, kp.name), writes=(ex.name,))
        p.emit("vector", lambda e: e.reduce_sum(z[:, :], ex[:, :], axis=X),
               reads=(ex.name,), writes=(z.name,))
        p.emit("vector", lambda e: e.reciprocal(mu2[:, :], z[:, :]),
               reads=(z.name,), writes=(mu2.name,))
        p.emit("vector", lambda e: e.tensor_scalar(eqm1[:, :], eqm1[:, :], -1.0,
                                                   1.0, ALU.mult, ALU.add),
               reads=(eqm1.name,), writes=(eqm1.name,))
        p.emit("vector", lambda e: e.tensor_scalar(eqm1[:, :], eqm1[:, :],
                                                   mu1[:, :], None, ALU.mult),
               reads=(eqm1.name, mu1.name), writes=(eqm1.name,))
        p.emit("vector", lambda e: e.tensor_scalar(eqm2[:, :], eqm2[:, :],
                                                   mu2[:, :], None, ALU.mult),
               reads=(eqm2.name, mu2.name), writes=(eqm2.name,))
        p.emit("vector", lambda e: e.tensor_tensor(wte[:, :], eqm1[:, :],
                                                   eqm2[:, :], ALU.add),
               reads=(eqm1.name, eqm2.name), writes=(wte.name,))
        p.emit("tensor", lambda e: e.transpose(wTps[:, :], wte[:, :],
                                               ctx.t("ident_sb")[:, :]),
               reads=(wte.name, "ident_sb", wTps.name), writes=(wTps.name,))
        p.emit("vector", lambda e, tsl=tsl: e.tensor_copy(wT[:, tsl], wTps[:, :]),
               reads=(wTps.name,), writes=(wT_name,))


def emit_l3_ffn(ctx, aps, xpfx="x"):
    p = ctx.prog
    emit_ln_full(ctx, xpfx)
    gw = p.sbuf("l3gw", [128, 8 * 8], BF16)
    idn = p.sbuf_shared("ident_sb", [128, 128], BF16)
    sel8 = p.sbuf_shared("sel8_sb", [8, 8 * 128], BF16)
    p.dma("gpsimd", gw[:, :], aps["gw"], writes=("l3gw",))
    p.dma("gpsimd", idn[:, :], aps["ident"], writes=("ident_sb",))
    p.dma("gpsimd", sel8[:, :], aps["sel8"], writes=("sel8_sb",))
    emit_phi_gate(ctx, "l3g", "l3gw", 8, "l3wT")
    h = ctx.hb
    wEb = p.sbuf("l3wEb", [128, 1024], BF16)
    sl1 = [p.sbuf(f"l3sl{j}", [128, 512], F32) for j in range(2)]
    wa = [p.sbuf(f"l3wa{k}", [128, 1024], BF16) for k in range(2)]
    wb = [p.sbuf(f"l3wb{k}", [128, 1024], BF16) for k in range(2)]
    for e_ in range(8):
        for np_ in range(NT // 2):
            ss = [slice((2 * np_ + j) * 512, (2 * np_ + j + 1) * 512)
                  for j in range(2)]
            for m in range(32):
                k = m % 2
                w1b = wa[k]
                p.dma("gpsimd", w1b[:, :], aps["w1"][e_, m], writes=(w1b.name,))
                for kc in range(NCH):
                    for j in range(2):
                        MM(ctx, ctx.pbig[j], w1b.name, ctx.xn_names[kc],
                           w_sl=(slice(None), slice(kc * 128, (kc + 1) * 128)),
                           x_sl=(slice(None), ss[j]),
                           start=(kc == 0), stop=(kc == NCH - 1))
                w3b = wb[k]
                p.dma("gpsimd", w3b[:, :], aps["w3"][e_, m], writes=(w3b.name,))
                for j in range(2):
                    ACT_(ctx, "Silu", sl1[j].name, ctx.pbig[j])
                for kc in range(NCH):
                    for j in range(2):
                        MM(ctx, ctx.pbig[2 + j], w3b.name, ctx.xn_names[kc],
                           w_sl=(slice(None), slice(kc * 128, (kc + 1) * 128)),
                           x_sl=(slice(None), ss[j]),
                           start=(kc == 0), stop=(kc == NCH - 1))
                for j in range(2):
                    p.emit("vector",
                           lambda e, m=m, j=j: e.tensor_tensor(
                               h[2 * m + j][:, :], sl1[j][:, :],
                               ctx.t(ctx.pbig[2 + j])[:, :], ALU.mult),
                           reads=(sl1[j].name, ctx.pbig[2 + j]),
                           writes=(h[2 * m + j].name,))
            for j in range(2):
                p.emit("tensor",
                       lambda e, e_=e_, j=j, ss=ss: e.matmul(
                           ctx.t(ctx.pbig[0])[:, :],
                           sel8[:, e_ * 128:(e_ + 1) * 128],
                           ctx.t("l3wT")[:, ss[j]], start=True, stop=True),
                       reads=("sel8_sb", "l3wT", ctx.pbig[0]),
                       writes=(ctx.pbig[0],))
                ACT_(ctx, "Copy", "l3wEb", ctx.pbig[0],
                     out_sl=(slice(None), slice(j * 512, (j + 1) * 512)))
            for mo in range(NCH):
                k = mo % 2
                w2s = ctx.ws[k]
                p.dma("gpsimd", w2s[:, :], aps["w2a"][e_, mo], writes=(w2s.name,))
                w2s2 = wa[k]
                p.dma("gpsimd", w2s2[:, 0:1024], aps["w2b"][e_, mo],
                      writes=(w2s2.name,))
                for kc in range(24):
                    for j in range(2):
                        MM(ctx, ctx.pbig[2 + j], w2s.name, h[2 * kc + j].name,
                           w_sl=(slice(None), slice(kc * 128, (kc + 1) * 128)),
                           start=(kc == 0), stop=False)
                for kc in range(24, 32):
                    for j in range(2):
                        MM(ctx, ctx.pbig[2 + j], w2s2.name, h[2 * kc + j].name,
                           w_sl=(slice(None), slice((kc - 24) * 128,
                                                    (kc - 23) * 128)),
                           start=False, stop=(kc == 31))
                t1 = ctx.fa[0]
                x = ctx.t(f"{xpfx}{mo}")
                for j in range(2):
                    p.emit("vector",
                           lambda e, j=j, t1=t1: e.tensor_tensor(
                               t1[:, :], ctx.t(ctx.pbig[2 + j])[:, :],
                               wEb[:, j * 512:(j + 1) * 512], ALU.mult),
                           reads=(ctx.pbig[2 + j], "l3wEb"), writes=(t1.name,))
                    p.emit("vector",
                           lambda e, x=x, j=j, ss=ss, t1=t1: e.tensor_tensor(
                               x[:, ss[j]], x[:, ss[j]], t1[:, :], ALU.add),
                           reads=(f"{xpfx}{mo}", t1.name), writes=(f"{xpfx}{mo}",))


def host_prep_l3_ffn(inputs):
    g = lambda k: np.ascontiguousarray(np.asarray(inputs[k]))
    gw = g("L3_ffn_gate_w")
    w1 = g("L3_ffn_w1")
    w2 = g("L3_ffn_w2")
    w3 = g("L3_ffn_w3")
    sel8 = np.zeros((8, 8 * 128), dtype=BFNP)
    for e in range(8):
        sel8[e, e * 128:(e + 1) * 128] = 1.0
    w1t = np.stack([host_tile_lhsT(np.ascontiguousarray(w1[e].T))
                    for e in range(8)])
    w3t = np.stack([host_tile_lhsT(np.ascontiguousarray(w3[e].T))
                    for e in range(8)])
    w2t = np.stack([host_tile_lhsT(np.ascontiguousarray(w2[e].T))
                    for e in range(8)])
    return {
        "gw": np.ascontiguousarray(gw.T.reshape(8, 128, 8).transpose(1, 0, 2)),
        "w1": w1t, "w3": w3t,
        "w2a": np.ascontiguousarray(w2t[:, :, :, 0:24, :]),
        "w2b": np.ascontiguousarray(w2t[:, :, :, 24:32, :]),
        "sel8": sel8,
        "ident": np.eye(128, dtype=BFNP),
    }


def build_l3_ffn_launch():
    p = Prog()
    x_ap = p.dram_in("x", [NCH, 128, TLOC], F32)
    aps = {
        "gw": p.dram_in("gw", [128, 8, 8], F32),
        "w1": p.dram_in("w1", [8, 32, 128, 8, 128], F32),
        "w3": p.dram_in("w3", [8, 32, 128, 8, 128], F32),
        "w2a": p.dram_in("w2a", [8, 8, 128, 24, 128], F32),
        "w2b": p.dram_in("w2b", [8, 8, 128, 8, 128], F32),
        "sel8": p.dram_in("sel8", [8, 8 * 128], BF16),
        "ident": p.dram_in("ident", [128, 128], BF16),
    }
    xo_ap = p.dram_out("xo", [NCH, 128, TLOC], F32)
    for c in range(NCH):
        p.sbuf(f"x{c}", [128, TLOC], F32)
    ctx = Ctx(p, n_hb=64, n_fa=2, n_ws=2, n_rows=5)
    p.sbuf("l3t1", [128, 512], F32)
    for c in range(NCH):
        p.dma("gpsimd", ctx.t(f"x{c}")[:, :], x_ap[c], writes=(f"x{c}",))
    emit_l3_ffn(ctx, aps)
    outs = []
    for c in range(NCH):
        outs.append(p.dma("gpsimd", xo_ap[c], ctx.t(f"x{c}")[:, :],
                          reads=(f"x{c}",), writes=(f"xo{c}",)))
    return p.build(output_dma_ops=outs)


def emit_l1_attn(ctx, aps, xpfx="x"):
    p = ctx.prog
    kf = [p.sbuf(f"l1k{c}", [128, TLOC], BF16) for c in range(NCH)]
    vf = [p.sbuf(f"l1v{c}", [128, TLOC], BF16) for c in range(NCH)]
    qf = [p.sbuf(f"l1q{c}", [128, TLOC // 2], BF16) for c in range(NCH)]
    bqkv = p.sbuf("l1b", [128, 3 * 8], F32)
    rt = p.sbuf("l1rt", [128, 64], BF16)
    p.dma("sync", bqkv[:, :], aps["b"], writes=("l1b",))
    p.dma("gpsimd", rt[:, :], aps["rt"], writes=("l1rt",))
    B1G = (slice(512, 1024), slice(1536, 2048))
    B0G = (slice(0, 512), slice(1024, 1536))
    xn_t = ctx.hb[0:8]
    xn_names = [t.name for t in xn_t]
    for nt in range(NT):
        s = slice(nt * 512, (nt + 1) * 512)
        is_b1 = (nt % 2 == 1)
        gi = (nt - 1) // 2
        emit_ln_tile(ctx, xpfx, nt, xn_names)
        for m in range(NCH):
            k = m % 2
            wl = (("wk", kf, 8, s), ("wv", vf, 16, s))
            if is_b1:
                wl = (("wq", qf, 0, slice(gi * 512, (gi + 1) * 512)),) + wl
            for wkey, dst, bofs, osl_ in wl:
                wsb = ctx.ws[k]
                p.dma("gpsimd", wsb[:, 0:1024], aps[wkey][m], writes=(wsb.name,))
                pg = ctx.pbig[k]
                for kc in range(NCH):
                    MM(ctx, pg, wsb.name, xn_names[kc],
                       w_sl=(slice(None), slice(kc * 128, (kc + 1) * 128)),
                       start=(kc == 0), stop=(kc == NCH - 1))
                ACT_(ctx, "Identity", dst[m].name, pg,
                     out_sl=(slice(None), osl_),
                     bias="l1b", bias_sl=(slice(None), slice(bofs + m, bofs + m + 1)))
    for h in range(16):
        c, r0 = h // 2, (h % 2) * 64
        hs = slice(r0, r0 + 64)
        for gi in range(2):
            qs = slice(gi * 512, (gi + 1) * 512)
            pg = ctx.pbig[2]
            p.emit("tensor",
                   lambda e, c=c, hs=hs, qs=qs, pg=pg: e.matmul(
                       ctx.t(pg)[0:64, :], rt[hs, :], qf[c][hs, qs],
                       start=True, stop=True),
               reads=("l1rt", qf[c].name, pg), writes=(pg,))
            p.emit("vector",
                   lambda e, c=c, hs=hs, qs=qs, pg=pg: e.tensor_copy(
                       qf[c][hs, qs], ctx.t(pg)[0:64, :]),
                   reads=(pg,), writes=(qf[c].name,))
            ks = B1G[gi]
            pg2 = ctx.pbig[3]
            p.emit("tensor",
                   lambda e, c=c, hs=hs, ks=ks, pg2=pg2: e.matmul(
                       ctx.t(pg2)[0:64, :], rt[hs, :], kf[c][hs, ks],
                       start=True, stop=True),
                   reads=("l1rt", kf[c].name, pg2), writes=(pg2,))
            p.emit("vector",
                   lambda e, c=c, hs=hs, ks=ks, pg2=pg2: e.tensor_copy(
                       kf[c][hs, ks], ctx.t(pg2)[0:64, :]),
                   reads=(pg2,), writes=(kf[c].name,))
    for c in range(NCH):
        x = ctx.t(f"{xpfx}{c}")
        for s in B0G:
            p.emit("vector",
                   lambda e, x=x, c=c, s=s: e.scalar_tensor_tensor(
                       x[:, s], vf[c][:, s], 0.5, x[:, s], ALU.mult, ALU.add),
                   reads=(vf[c].name, f"{xpfx}{c}"), writes=(f"{xpfx}{c}",))
    prod = p.sbuf("l1prod", [128, 512], F32)
    r0_ = p.sbuf("l1r0", [1, 512], F32)
    r1_ = p.sbuf("l1r1", [1, 512], F32)
    rm = p.sbuf("l1rm", [1, 512], F32)
    re0 = p.sbuf("l1re0", [1, 512], BF16)
    re1 = p.sbuf("l1re1", [1, 512], BF16)
    rden = p.sbuf("l1rden", [1, 512], F32)
    tmpo = p.sbuf("l1tmpo", [128, 512], F32)
    for h in range(16):
        c, r0 = h // 2, (h % 2) * 64
        hs = slice(r0, r0 + 64)
        for gi in range(2):
            qs = slice(gi * 512, (gi + 1) * 512)
            s0, s1 = B0G[gi], B1G[gi]
            for which, ssl, rout in ((0, s0, "l1r0"), (1, s1, "l1r1")):
                p.emit("vector",
                       lambda e, c=c, hs=hs, qs=qs, ssl=ssl: e.tensor_tensor(
                           prod[hs, :], qf[c][hs, qs], kf[c][hs, ssl], ALU.mult),
                       reads=(qf[c].name, kf[c].name), writes=("l1prod",))
                p.emit("tensor",
                       lambda e, rout=rout, hs=hs: e.matmul(
                           ctx.t(ctx.prow0)[:, :], ctx.ones_col_f[hs, :],
                           prod[hs, :], start=True, stop=True),
                       reads=("l1prod", "ones_col_f", ctx.prow0), writes=(ctx.prow0,))
                ACT_(ctx, "Copy", rout, ctx.prow0, scale=0.125)
            VOP(ctx, "max", "l1rm", "l1r0", "l1r1")
            VOP(ctx, "subtract", "l1r0", "l1r0", "l1rm")
            VOP(ctx, "subtract", "l1r1", "l1r1", "l1rm")
            ACT_(ctx, "Exp", "l1re0", "l1r0")
            ACT_(ctx, "Exp", "l1re1", "l1r1")
            VOP(ctx, "add", "l1rden", "l1re0", "l1re1")
            p.emit("vector", lambda e: e.tensor_scalar_add(rden[:, :], rden[:, :],
                                                           1.0),
                   reads=("l1rden",), writes=("l1rden",))
            p.emit("vector", lambda e: e.reciprocal(rden[:, :], rden[:, :]),
                   reads=("l1rden",), writes=("l1rden",))
            p.emit("vector", lambda e: e.tensor_tensor(
                re0[:, :], re0[:, :], rden[:, :], ALU.mult),
                   reads=("l1re0", "l1rden"), writes=("l1re0",))
            p.emit("vector", lambda e: e.tensor_tensor(
                re1[:, :], re1[:, :], rden[:, :], ALU.mult),
                   reads=("l1re1", "l1rden"), writes=("l1re1",))
            pg = ctx.pbig[2]
            p.emit("tensor", lambda e, pg=pg: e.matmul(
                ctx.t(pg)[0:64, :], ctx.ones_row_b[:, 0:64], re0[:, :],
                start=True, stop=True),
                   reads=("l1re0", "ones_row_b", pg), writes=(pg,))
            pg2 = ctx.pbig[3]
            p.emit("tensor", lambda e, pg2=pg2: e.matmul(
                ctx.t(pg2)[0:64, :], ctx.ones_row_b[:, 0:64], re1[:, :],
                start=True, stop=True),
                   reads=("l1re1", "ones_row_b", pg2), writes=(pg2,))
            p.emit("vector",
                   lambda e, c=c, hs=hs, s0=s0, pg=pg: e.tensor_tensor(
                       tmpo[hs, :], vf[c][hs, s0], ctx.t(pg)[0:64, :], ALU.mult),
                   reads=(vf[c].name, pg), writes=("l1tmpo",))
            p.emit("vector",
                   lambda e, c=c, hs=hs, s1=s1, pg2=pg2: e.tensor_tensor(
                       prod[hs, :], vf[c][hs, s1], ctx.t(pg2)[0:64, :], ALU.mult),
                   reads=(vf[c].name, pg2), writes=("l1prod",))
            p.emit("vector",
                   lambda e, hs=hs: e.tensor_tensor(
                       tmpo[hs, :], tmpo[hs, :], prod[hs, :], ALU.add),
                   reads=("l1tmpo", "l1prod"), writes=("l1tmpo",))
            x = ctx.t(f"{xpfx}{c}")
            p.emit("vector",
                   lambda e, x=x, hs=hs, s1=s1: e.tensor_tensor(
                       x[hs, s1], x[hs, s1], tmpo[hs, :], ALU.add),
                   reads=(f"{xpfx}{c}", "l1tmpo"), writes=(f"{xpfx}{c}",))


def host_rope_rt():
    Dk = 64
    ar = np.arange(0, Dk, 2, dtype=np.float32)
    with np.errstate(over="ignore"):
        powv = np.power(np.float32(10000.0), ar, dtype=np.float32)
    theta = (np.float32(1.0) / (powv / np.float32(Dk))).astype(np.float32)
    idx = theta
    idx2 = np.concatenate([idx, idx])
    cos, sin = np.cos(idx2), np.sin(idx2)
    R = np.zeros((64, 64), dtype=np.float32)
    for i in range(64):
        R[i, i] = cos[i]
        if i < 32:
            R[i, i + 32] = -sin[i]
        else:
            R[i, i - 32] = sin[i]
    RT = np.ascontiguousarray(R.T).astype(BFNP)
    return np.ascontiguousarray(np.vstack([RT, RT]))


def host_prep_l1_attn(inputs):
    g = lambda k: np.ascontiguousarray(np.asarray(inputs[k]))
    b = np.stack([g("L1_attn_bq").reshape(8, 128).T,
                  g("L1_attn_bk").reshape(8, 128).T,
                  g("L1_attn_bv").reshape(8, 128).T], axis=1)
    return {
        "wq": host_tile_lhsT(np.ascontiguousarray(g("L1_attn_Wq").T)),
        "wk": host_tile_lhsT(np.ascontiguousarray(g("L1_attn_Wk").T)),
        "wv": host_tile_lhsT(np.ascontiguousarray(g("L1_attn_Wv").T)),
        "b": np.ascontiguousarray(b),
        "rt": host_rope_rt(),
    }


def build_l1_attn_launch():
    p = Prog()
    x_ap = p.dram_in("x", [NCH, 128, TLOC], F32)
    aps = {
        "wq": p.dram_in("wq", [8, 128, 8, 128], F32),
        "wk": p.dram_in("wk", [8, 128, 8, 128], F32),
        "wv": p.dram_in("wv", [8, 128, 8, 128], F32),
        "b": p.dram_in("b", [128, 3, 8], F32),
        "rt": p.dram_in("rt", [128, 64], BF16),
    }
    xo_ap = p.dram_out("xo", [NCH, 128, TLOC], F32)
    for c in range(NCH):
        p.sbuf(f"x{c}", [128, TLOC], F32)
    ctx = Ctx(p, n_hb=8, n_fa=2, n_ws=2, with_xn=False)
    for c in range(NCH):
        p.dma("gpsimd", ctx.t(f"x{c}")[:, :], x_ap[c], writes=(f"x{c}",))
    emit_l1_attn(ctx, aps)
    outs = []
    for c in range(NCH):
        outs.append(p.dma("gpsimd", xo_ap[c], ctx.t(f"x{c}")[:, :],
                          reads=(f"x{c}",), writes=(f"xo{c}",)))
    return p.build(output_dma_ops=outs)


def emit_l3_attn_pre(ctx, aps, pq_out, kv_out, xpfx="x"):
    p = ctx.prog
    rf = p.sbuf("l3rf", [128, 16 * 64], BF16)
    p.dma("gpsimd", rf[:, :], aps["rf"], writes=("l3rf",))
    xn_t = ctx.hb[0:8]
    xn_names = [t.name for t in xn_t]
    kt = [p.sbuf(f"l3k{c}", [128, 512], BF16) for c in range(NCH)]
    pkT = [p.sbuf(f"l3pkT{t}", [128, 1024], BF16) for t in range(4)]
    vtok = [p.sbuf(f"l3vt{t}", [128, 16 * 65], BF16) for t in range(4)]
    kvloc = p.sbuf("l3kvloc", [64, 2 * 16 * 65], F32)
    stg = p.sbuf("l3stg", [128, 512], BF16)
    pkv = p.psum("pkv", [64, 65], F32)
    p.emit("vector", lambda e: e.memset(kvloc[:, :], 0.0), writes=("l3kvloc",))
    for nt in range(NT):
        s = slice(nt * 512, (nt + 1) * 512)
        b = nt % 2
        emit_ln_tile(ctx, xpfx, nt, xn_names)
        for m in range(NCH):
            k = m % 2
            wsb = ctx.ws[k]
            p.dma("gpsimd", wsb[:, 0:1024], aps["wqkv"][m], writes=(wsb.name,))
            pg = ctx.pbig[k]
            for kc in range(NCH):
                MM(ctx, pg, wsb.name, xn_names[kc],
                   w_sl=(slice(None), slice(kc * 128, (kc + 1) * 128)),
                   start=(kc == 0), stop=(kc == NCH - 1))
            ACT_(ctx, "Copy", kt[m].name, pg)
            for hh in range(2):
                h = 2 * m + hh
                hs = slice(hh * 64, (hh + 1) * 64)
                pg2 = ctx.pbig[2 + (hh % 2)]
                p.emit("tensor",
                       lambda e, m=m, hs=hs, h=h, pg2=pg2: e.matmul(
                           ctx.t(pg2)[0:64, :], rf[hs, h * 64:(h + 1) * 64],
                           kt[m][hs, :], start=True, stop=True),
                       reads=("l3rf", kt[m].name, pg2), writes=(pg2,))
                ACT_(ctx, "Relu", stg.name, pg2,
                     in_sl=(slice(0, 64), slice(None)),
                     out_sl=(slice(0, 64), slice(None)))
                ctx.last_pq = p.dma("gpsimd", pq_out[m, hs, s], stg[0:64, :],
                                    reads=(stg.name,), writes=("pqo",))
        for m in range(NCH):
            k = m % 2
            wsb = ctx.ws[k]
            p.dma("gpsimd", wsb[:, 0:1024], aps["wqkv"][8 + m], writes=(wsb.name,))
            pg = ctx.pbig[k]
            for kc in range(NCH):
                MM(ctx, pg, wsb.name, xn_names[kc],
                   w_sl=(slice(None), slice(kc * 128, (kc + 1) * 128)),
                   start=(kc == 0), stop=(kc == NCH - 1))
            ACT_(ctx, "Copy", kt[m].name, pg)
        for h in range(16):
            c, hh = h // 2, h % 2
            hs = slice(hh * 64, (hh + 1) * 64)
            for tt in range(4):
                pg2 = ctx.pbig[2 + (tt % 2)]
                p.emit("tensor",
                       lambda e, c=c, hs=hs, h=h, tt=tt, pg2=pg2: e.matmul(
                           ctx.t(pg2)[:, 0:64], kt[c][hs, tt * 128:(tt + 1) * 128],
                           rf[hs, h * 64:(h + 1) * 64], start=True, stop=True),
                       reads=(kt[c].name, "l3rf", pg2), writes=(pg2,))
                p.emit("scalar",
                       lambda e, h=h, tt=tt, pg2=pg2: e.activation(
                           out=pkT[tt][:, h * 64:(h + 1) * 64],
                           in_=ctx.t(pg2)[:, 0:64], func=AF.Relu),
                       reads=(pg2,), writes=(pkT[tt].name,))
        for tt in range(4):
            p.emit("vector", lambda e, tt=tt: e.memset(vtok[tt][:, :], 1.0),
                   writes=(vtok[tt].name,))
        for m in range(NCH):
            k = m % 2
            wsb = ctx.ws[k]
            p.dma("gpsimd", wsb[:, 0:1024], aps["wqkv"][16 + m], writes=(wsb.name,))
            for tt in range(4):
                pg = ctx.pbig[tt % 2]
                for kc in range(NCH):
                    MM(ctx, pg, xn_names[kc], wsb.name,
                       w_sl=(slice(None), slice(tt * 128, (tt + 1) * 128)),
                       x_sl=(slice(None), slice(kc * 128, (kc + 1) * 128)),
                       ps_sl=(slice(None), slice(0, 128)),
                       start=(kc == 0), stop=(kc == NCH - 1))
                for hh in range(2):
                    h = 2 * m + hh
                    p.emit("vector",
                           lambda e, tt=tt, h=h, hh=hh, pg=pg: e.tensor_copy(
                               vtok[tt][:, h * 65:h * 65 + 64],
                               ctx.t(pg)[:, hh * 64:(hh + 1) * 64]),
                           reads=(pg,), writes=(vtok[tt].name,))
        for h in range(16):
            for tt in range(4):
                p.emit("tensor",
                       lambda e, h=h, tt=tt: e.matmul(
                           pkv[:, :], pkT[tt][:, h * 64:(h + 1) * 64],
                           vtok[tt][:, h * 65:(h + 1) * 65],
                           start=(tt == 0), stop=(tt == 3)),
                       reads=(pkT[tt].name, vtok[tt].name, "pkv"),
                       writes=("pkv",))
            ksl = slice((b * 16 + h) * 65, (b * 16 + h + 1) * 65)
            p.emit("vector",
                   lambda e, ksl=ksl: e.tensor_tensor(
                       kvloc[:, ksl], kvloc[:, ksl], pkv[:, :], ALU.add),
                   reads=("l3kvloc", "pkv"), writes=("l3kvloc",))
    return p.dma("gpsimd", kv_out, kvloc[:, :], reads=("l3kvloc",),
                 writes=("kvo",))


def emit_l3_attn_post(ctx, aps, xpfx="x"):
    p = ctx.prog
    pq = [p.sbuf(f"l3pq{c}", [128, TLOC], BF16) for c in range(NCH)]
    ao = [p.sbuf(f"l3ao{c}", [128, TLOC], BF16) for c in range(NCH)]
    acc = p.sbuf("l3acc", [64, 2 * 16 * 65], F32)
    stg = p.sbuf("l3kstg", [64, 2 * 16 * 65], F32)
    kvb = p.sbuf("l3kvb", [128, 65], BF16)
    bo = p.sbuf("l3bo", [128, 8], F32)
    rden = p.sbuf("l3rden", [1, 512], F32)
    nsb = p.sbuf("l3nsb", [64, 512], F32)
    pnd = p.psum("pnd", [65, 512], F32)
    pbc = p.psum("pbc", [64, 512], F32)
    p.dma("sync", bo[:, :], aps["bo"], writes=("l3bo",))
    for c in range(NCH):
        p.dma("gpsimd", pq[c][:, :], aps["pq"][c], writes=(pq[c].name,))
    for i in range(NCORES):
        src_ap = (aps["kvall"][i] if not aps.get("kv_2d")
                  else aps["kvall"][i * 64:(i + 1) * 64, :])
        if i == 0:
            p.dma("sync", acc[:, :], src_ap, writes=("l3acc",))
        else:
            p.dma("sync", stg[:, :], src_ap, writes=("l3kstg",))
            VOP(ctx, "add", "l3acc", "l3acc", "l3kstg")
    GRP = {0: (slice(0, 512), slice(1024, 1536)),
           1: (slice(512, 1024), slice(1536, 2048))}
    for b in range(2):
        for h in range(16):
            c, hh = h // 2, h % 2
            hs = slice(hh * 64, (hh + 1) * 64)
            ksl = slice((b * 16 + h) * 65, (b * 16 + h + 1) * 65)
            p.emit("vector", lambda e, ksl=ksl: e.tensor_copy(
                kvb[0:64, :], acc[:, ksl]),
                   reads=("l3acc",), writes=("l3kvb",))
            p.emit("vector", lambda e, ksl=ksl: e.tensor_copy(
                kvb[64:128, :], acc[:, ksl]),
                   reads=("l3acc",), writes=("l3kvb",))
            for g in GRP[b]:
                p.emit("tensor",
                       lambda e, hs=hs, c=c, g=g: e.matmul(
                           pnd[:, :], kvb[hs, :] if hs.start == 64 else kvb[0:64, :],
                           pq[c][hs, g], start=True, stop=True),
                       reads=("l3kvb", pq[c].name, "pnd"), writes=("pnd",))
                p.emit("vector", lambda e: e.tensor_copy(rden[:, :],
                                                         pnd[64:65, :]),
                       reads=("pnd",), writes=("l3rden",))
                p.emit("vector", lambda e: e.tensor_scalar_add(
                    rden[:, :], rden[:, :], 1e-6),
                       reads=("l3rden",), writes=("l3rden",))
                p.emit("vector", lambda e: e.reciprocal(rden[:, :], rden[:, :]),
                       reads=("l3rden",), writes=("l3rden",))
                p.emit("tensor", lambda e: e.matmul(
                    pbc[:, :], ctx.ones_row_f[:, 0:64], rden[:, :],
                    start=True, stop=True),
                       reads=("l3rden", "ones_row_f", "pbc"), writes=("pbc",))
                ACT_(ctx, "Copy", "l3nsb", "pnd",
                     in_sl=(slice(0, 64), slice(None)))
                p.emit("vector",
                       lambda e, c=c, hs=hs, g=g: e.tensor_tensor(
                           ao[c][hs, g], nsb[:, :], pbc[:, :], ALU.mult),
                       reads=("l3nsb", "pbc"), writes=(ao[c].name,))
    for mo in range(NCH):
        k = mo % 2
        wsb = ctx.ws[k]
        p.dma("gpsimd", wsb[:, 0:1024], aps["wout"][mo], writes=(wsb.name,))
        for nt in range(NT):
            s = slice(nt * 512, (nt + 1) * 512)
            pg = ctx.pbig[nt % 2]
            for kc in range(NCH):
                MM(ctx, pg, wsb.name, ao[kc].name,
                   w_sl=(slice(None), slice(kc * 128, (kc + 1) * 128)),
                   x_sl=(slice(None), s),
                   start=(kc == 0), stop=(kc == NCH - 1))
            x = ctx.t(f"{xpfx}{mo}")
            p.emit("vector",
                   lambda e, x=x, s=s, pg=pg, mo=mo: e.scalar_tensor_tensor(
                       x[:, s], ctx.t(pg)[:, :], bo[:, mo:mo + 1], x[:, s],
                       ALU.add, ALU.add),
                   reads=(pg, f"{xpfx}{mo}", "l3bo"), writes=(f"{xpfx}{mo}",))


def host_prep_l3_attn(inputs):
    g = lambda k: np.ascontiguousarray(np.asarray(inputs[k]))
    rfm = g("L3_attn_RF")
    rf = np.concatenate([rfm, rfm], axis=1).transpose(1, 0, 2).reshape(128, 16 * 64)
    return {
        "wqkv": host_tile_lhsT(g("L3_attn_W_qkv")),
        "rf": np.ascontiguousarray(rf),
        "wout": host_tile_lhsT(g("L3_attn_W_out")),
        "bo": np.ascontiguousarray(g("L3_attn_b_out").reshape(8, 128).T),
    }


def build_l3_attn_pre_launch():
    p = Prog()
    x_ap = p.dram_in("x", [NCH, 128, TLOC], F32)
    aps = {
        "wqkv": p.dram_in("wqkv", [24, 128, 8, 128], F32),
        "rf": p.dram_in("rf", [128, 16 * 64], F32),
    }
    pq_out = p.dram_out("pq", [NCH, 128, TLOC], BF16)
    kv_out = p.dram_out("kv", [64, 2 * 16 * 65], F32)
    for c in range(NCH):
        p.sbuf(f"x{c}", [128, TLOC], F32)
    ctx = Ctx(p, n_hb=8, n_fa=2, n_ws=2, with_xn=False)
    for c in range(NCH):
        p.dma("gpsimd", ctx.t(f"x{c}")[:, :], x_ap[c], writes=(f"x{c}",))
    kv_dma = emit_l3_attn_pre(ctx, aps, pq_out, kv_out)
    return p.build(output_dma_ops=[kv_dma, ctx.last_pq])


def build_l3_attn_post_launch():
    p = Prog()
    x_ap = p.dram_in("x", [NCH, 128, TLOC], F32)
    aps = {
        "pq": p.dram_in("pqin", [NCH, 128, TLOC], BF16),
        "kvall": p.dram_in("kvall", [NCORES, 64, 2 * 16 * 65], F32),
        "wout": p.dram_in("wout", [8, 128, 8, 128], F32),
        "bo": p.dram_in("bo", [128, 8], F32),
    }
    xo_ap = p.dram_out("xo", [NCH, 128, TLOC], F32)
    for c in range(NCH):
        p.sbuf(f"x{c}", [128, TLOC], F32)
    ctx = Ctx(p, n_hb=0, n_fa=2, n_ws=2, with_xn=False)
    for c in range(NCH):
        p.dma("gpsimd", ctx.t(f"x{c}")[:, :], x_ap[c], writes=(f"x{c}",))
    emit_l3_attn_post(ctx, aps)
    outs = []
    for c in range(NCH):
        outs.append(p.dma("gpsimd", xo_ap[c], ctx.t(f"x{c}")[:, :],
                          reads=(f"x{c}",), writes=(f"xo{c}",)))
    return p.build(output_dma_ops=outs)


def _emit_phi2(ctx, psname, out_t, out_sl, fa0, fa1):
    p = ctx.prog
    ACT_(ctx, "Relu", fa0.name, psname)
    p.emit("vector", lambda e: e.tensor_scalar_min(fa1[:, :],
                                                   ctx.t(psname)[:, :], 0.0),
           reads=(psname,), writes=(fa1.name,))
    ACT_(ctx, "Exp", fa1.name, fa1.name)
    p.emit("vector", lambda e: e.tensor_tensor(
        out_t[out_sl] if out_sl else out_t[:, :], fa0[:, :], fa1[:, :], ALU.add),
           reads=(fa0.name, fa1.name), writes=(out_t.name,))


def emit_l2_attn_pre(ctx, aps, q_out, k_out, v_out, st_out, xpfx="x"):
    p = ctx.prog
    bqkv = p.sbuf("l2b", [128, 2 * 8], F32)
    id64 = p.sbuf_shared("id64_sb", [128, 64], BF16)
    p.dma("sync", bqkv[:, :], aps["b"], writes=("l2b",))
    p.dma("gpsimd", id64[:, :], aps["id64"], writes=("id64_sb",))
    xn_t = ctx.hb[0:8]
    xn_names = [t.name for t in xn_t]
    kt = [p.sbuf(f"l2k{c}", [128, 512], BF16) for c in range(NCH)]
    pkT = [p.sbuf(f"l2pkT{t}", [128, 1024], BF16) for t in range(4)]
    vtok = [p.sbuf(f"l2vt{t}", [128, 16 * 65], BF16) for t in range(4)]
    stloc = p.sbuf("l2stloc", [64, 4 * 16 * 65], F32)
    stgq = [p.sbuf(f"l2stg{k}", [128, 512], BF16) for k in range(2)]
    ptr = p.psum("ptr", [128, 64], BF16)
    pkv = p.psum("pkv2", [64, 65], F32)
    for nt in range(NT):
        s = slice(nt * 512, (nt + 1) * 512)
        emit_ln_tile(ctx, xpfx, nt, xn_names)
        for m in range(NCH):
            k = m % 2
            wsb = ctx.ws[k]
            p.dma("gpsimd", wsb[:, 0:1024], aps["wq"][m], writes=(wsb.name,))
            pg = ctx.pbig[k]
            for kc in range(NCH):
                MM(ctx, pg, wsb.name, xn_names[kc],
                   w_sl=(slice(None), slice(kc * 128, (kc + 1) * 128)),
                   start=(kc == 0), stop=(kc == NCH - 1))
            ACT_(ctx, "Identity", ctx.fa[4 + k].name, pg, bias="l2b",
                 bias_sl=(slice(None), slice(m, m + 1)))
            _emit_phi2(ctx, ctx.fa[4 + k].name, stgq[k], None,
                       ctx.fa[2 * k], ctx.fa[2 * k + 1])
            ctx.last_q = p.dma("gpsimd", q_out[m, :, s], stgq[k][:, :],
                               reads=(stgq[k].name,), writes=("qo",))
        for m in range(NCH):
            k = m % 2
            wsb = ctx.ws[k]
            p.dma("gpsimd", wsb[:, 0:1024], aps["wk"][m], writes=(wsb.name,))
            pg = ctx.pbig[k]
            for kc in range(NCH):
                MM(ctx, pg, wsb.name, xn_names[kc],
                   w_sl=(slice(None), slice(kc * 128, (kc + 1) * 128)),
                   start=(kc == 0), stop=(kc == NCH - 1))
            ACT_(ctx, "Identity", ctx.fa[4 + k].name, pg, bias="l2b",
                 bias_sl=(slice(None), slice(8 + m, 8 + m + 1)))
            _emit_phi2(ctx, ctx.fa[4 + k].name, kt[m], None,
                       ctx.fa[2 * k], ctx.fa[2 * k + 1])
            ctx.last_k = p.dma("gpsimd", k_out[m, :, s], kt[m][:, :],
                               reads=(kt[m].name,), writes=("ko",))
        for h in range(16):
            c, hh = h // 2, h % 2
            hs = slice(hh * 64, (hh + 1) * 64)
            for tt in range(4):
                p.emit("tensor",
                       lambda e, c=c, hs=hs, tt=tt: e.transpose(
                           ptr[:, :], kt[c][hs, tt * 128:(tt + 1) * 128],
                           id64[hs, :]),
                       reads=(kt[c].name, "id64_sb", "ptr"), writes=("ptr",))
                p.emit("vector",
                       lambda e, h=h, tt=tt: e.tensor_copy(
                           pkT[tt][:, h * 64:(h + 1) * 64], ptr[:, :]),
                       reads=("ptr",), writes=(pkT[tt].name,))
        for tt in range(4):
            p.emit("vector", lambda e, tt=tt: e.memset(vtok[tt][:, :], 1.0),
                   writes=(vtok[tt].name,))
        for m in range(NCH):
            k = m % 2
            wsb = ctx.ws[k]
            p.dma("gpsimd", wsb[:, 0:1024], aps["wv"][m], writes=(wsb.name,))
            for tt in range(4):
                pg = ctx.pbig[tt % 2]
                for kc in range(NCH):
                    MM(ctx, pg, xn_names[kc], wsb.name,
                       w_sl=(slice(None), slice(tt * 128, (tt + 1) * 128)),
                       x_sl=(slice(None), slice(kc * 128, (kc + 1) * 128)),
                       ps_sl=(slice(None), slice(0, 128)),
                       start=(kc == 0), stop=(kc == NCH - 1))
                for hh in range(2):
                    h = 2 * m + hh
                    p.emit("vector",
                           lambda e, tt=tt, h=h, hh=hh, pg=pg, m=m: e.scalar_tensor_tensor(
                               vtok[tt][:, h * 65:h * 65 + 64],
                               ctx.t(pg)[:, hh * 64:(hh + 1) * 64], 1.0,
                               ctx.t("l2bvb")[:, (2 * m + hh) * 64:(2 * m + hh + 1) * 64],
                               ALU.mult, ALU.add),
                           reads=(pg, "l2bvb"), writes=(vtok[tt].name,))
        for tt in range(4):
            ctx.last_v = p.dma("gpsimd", v_out[nt * 4 + tt], vtok[tt][:, :],
                               reads=(vtok[tt].name,), writes=("vo",))
        for h in range(16):
            for tt in range(4):
                p.emit("tensor",
                       lambda e, h=h, tt=tt: e.matmul(
                           pkv[:, :], pkT[tt][:, h * 64:(h + 1) * 64],
                           vtok[tt][:, h * 65:(h + 1) * 65],
                           start=(tt == 0), stop=(tt == 3)),
                       reads=(pkT[tt].name, vtok[tt].name, "pkv2"),
                       writes=("pkv2",))
            ksl = slice((nt * 16 + h) * 65, (nt * 16 + h + 1) * 65)
            p.emit("scalar",
                   lambda e, ksl=ksl: e.activation(out=stloc[:, ksl],
                                                   in_=pkv[:, :], func=AF.Copy),
                   reads=("pkv2",), writes=("l2stloc",))
    return p.dma("gpsimd", st_out, stloc[:, :], reads=("l2stloc",),
                 writes=("sto",))


def emit_l2_attn_post(ctx, aps, xpfx="x", ag=False):
    p = ctx.prog
    ctx._l2ag = ag
    wos = [p.sbuf(f"l2wos{k}", [128, 1024], BF16) for k in range(2)]
    qf = [p.sbuf(f"l2qf{c}", [128, TLOC], BF16) for c in range(NCH)]
    kf = [p.sbuf(f"l2kf{c}", [128, TLOC], BF16) for c in range(NCH)]
    vtok = [p.sbuf(f"l2pv{t}", [128, 16 * 65], BF16) for t in range(16)]
    ao = qf
    stage = p.sbuf("l2stage", [64, (4 * 16 * 65) if ctx._l2ag else (16 * 65)],
                   BF16 if ctx._l2ag else F32)
    kvpre = p.sbuf("l2kvpre", [128, 65], BF16)
    kvacc = None if ag else p.sbuf("l2kvacc", [64, 65], F32)
    pmask = p.sbuf("l2pmask", [64, 2 * 16], F32)
    cmask = p.sbuf("l2cmask", [128, 4 * 512], BF16)
    bo = p.sbuf("l2bo", [128, 8], F32)
    prb = p.sbuf("l2prb", [128, 512], BF16)
    rden = p.sbuf("l2rden", [1, 512], F32)
    nsb = p.sbuf("l2nsb", [64, 512], F32)
    pnd = p.psum("pnd2", [65, 512], F32)
    pbc = p.psum("pbc2", [64, 512], F32)
    kvaccs = p.sbuf("l2kvaccs", [64, 2 * 2 * 16 * 65], F32) if ag else None
    if ag:
        p.dma("sync", pmask[:, :], aps["pmask2"], writes=("l2pmask",))
    else:
        p.dma("sync", pmask[:, :], aps["pmask"], writes=("l2pmask",))
    p.dma("gpsimd", cmask[:, :], aps["cmask"], writes=("l2cmask",))
    p.dma("sync", bo[:, :], aps["bo"], writes=("l2bo",))
    for c in range(NCH):
        p.dma("gpsimd", qf[c][:, :], aps["q"][c], writes=(qf[c].name,))
        p.dma("gpsimd", kf[c][:, :], aps["k"][c], writes=(kf[c].name,))
    for t in range(16):
        p.dma("gpsimd", vtok[t][:, :], aps["v"][t], writes=(vtok[t].name,))
    if ag:
        p.emit("vector", lambda e: e.memset(kvaccs[:, :], 0.0),
               writes=("l2kvaccs",))
        for r in range(NCORES):
            p.dma("gpsimd", stage[:, :], aps["stall_ag"][r * 64:(r + 1) * 64, :],
                  writes=("l2stage",))
            for si in range(2):
                for slot in range(2):
                    for b in range(2):
                        for h in range(16):
                            ssl = slice(((si * 2 + b) * 16 + h) * 65,
                                        ((si * 2 + b) * 16 + h + 1) * 65)
                            dsl = slice((((slot * 2) + b) * 16 + h) * 65,
                                        (((slot * 2) + b) * 16 + h + 1) * 65)
                            mi = slot * 16 + r * 2 + si
                            p.emit("vector",
                                   lambda e, ssl=ssl, dsl=dsl, mi=mi:
                                   e.scalar_tensor_tensor(
                                       kvaccs[:, dsl], stage[:, ssl],
                                       pmask[:, mi:mi + 1], kvaccs[:, dsl],
                                       ALU.mult, ALU.add),
                                   reads=("l2stage", "l2pmask", "l2kvaccs"),
                                   writes=("l2kvaccs",))
    for b in range(2):
        for h in range(16):
            c, hh = h // 2, h % 2
            hs = slice(hh * 64, (hh + 1) * 64)
            if not ag:
                p.dma("sync", stage[:, :], aps["st"][b, h], writes=("l2stage",))
            for slot in range(2):
                g = slice((2 * slot + b) * 512, (2 * slot + b + 1) * 512)
                if ag:
                    asl = slice((((slot * 2) + b) * 16 + h) * 65,
                                (((slot * 2) + b) * 16 + h + 1) * 65)
                    p.emit("vector", lambda e, asl=asl: e.tensor_copy(
                        kvpre[0:64, :], kvaccs[:, asl]),
                           reads=("l2kvaccs",), writes=("l2kvpre",))
                    p.emit("vector", lambda e, asl=asl: e.tensor_copy(
                        kvpre[64:128, :], kvaccs[:, asl]),
                           reads=("l2kvaccs",), writes=("l2kvpre",))
                else:
                    p.emit("vector", lambda e: e.memset(kvacc[:, :], 0.0),
                           writes=("l2kvacc",))
                    for blk in range(NBLK):
                        p.emit("vector",
                               lambda e, blk=blk, slot=slot: e.scalar_tensor_tensor(
                                   kvacc[:, :], stage[:, blk * 65:(blk + 1) * 65],
                                   pmask[:, slot * 16 + blk:slot * 16 + blk + 1],
                                   kvacc[:, :], ALU.mult, ALU.add),
                               reads=("l2stage", "l2pmask", "l2kvacc"),
                               writes=("l2kvacc",))
                    p.emit("vector", lambda e: e.tensor_copy(kvpre[0:64, :],
                                                             kvacc[:, :]),
                           reads=("l2kvacc",), writes=("l2kvpre",))
                    p.emit("vector", lambda e: e.tensor_copy(kvpre[64:128, :],
                                                             kvacc[:, :]),
                           reads=("l2kvacc",), writes=("l2kvpre",))
                p.emit("tensor",
                       lambda e, hs=hs, c=c, g=g: e.matmul(
                           pnd[:, :], kvpre[hs, :], qf[c][hs, g],
                           start=True, stop=False),
                       reads=("l2kvpre", qf[c].name, "pnd2"), writes=("pnd2",))
                gt0 = (2 * slot + b) * 4
                for jt in range(4):
                    pg = ctx.pbig[jt % 2]
                    jsl = slice(g.start + jt * 128, g.start + (jt + 1) * 128)
                    p.emit("tensor",
                           lambda e, hs=hs, c=c, jsl=jsl, g=g, pg=pg: e.matmul(
                               ctx.t(pg)[:, :], kf[c][hs, jsl], qf[c][hs, g],
                               start=True, stop=True),
                           reads=(kf[c].name, qf[c].name, pg), writes=(pg,))
                    p.emit("vector",
                           lambda e, jt=jt, pg=pg: e.tensor_tensor(
                               prb[:, :], ctx.t(pg)[:, :],
                               cmask[:, jt * 512:(jt + 1) * 512], ALU.mult),
                           reads=(pg, "l2cmask"), writes=("l2prb",))
                    p.emit("tensor",
                           lambda e, jt=jt, h=h, gt0=gt0: e.matmul(
                               pnd[:, :], vtok[gt0 + jt][:, h * 65:(h + 1) * 65],
                               prb[:, :], start=False, stop=(jt == 3)),
                           reads=(vtok[gt0 + jt].name, "l2prb", "pnd2"),
                           writes=("pnd2",))
                p.emit("vector", lambda e: e.tensor_copy(rden[:, :],
                                                         pnd[64:65, :]),
                       reads=("pnd2",), writes=("l2rden",))
                p.emit("vector", lambda e: e.tensor_scalar_add(
                    rden[:, :], rden[:, :], 1e-6),
                       reads=("l2rden",), writes=("l2rden",))
                p.emit("vector", lambda e: e.reciprocal(rden[:, :], rden[:, :]),
                       reads=("l2rden",), writes=("l2rden",))
                p.emit("tensor", lambda e: e.matmul(
                    pbc[:, :], ctx.ones_row_f[:, 0:64], rden[:, :],
                    start=True, stop=True),
                       reads=("l2rden", "ones_row_f", "pbc2"), writes=("pbc2",))
                ACT_(ctx, "Copy", "l2nsb", "pnd2",
                     in_sl=(slice(0, 64), slice(None)))
                p.emit("vector",
                       lambda e, c=c, hs=hs, g=g: e.tensor_tensor(
                           ao[c][hs, g], nsb[:, :], pbc[:, :], ALU.mult),
                       reads=("l2nsb", "pbc2"), writes=(ao[c].name,))
    for mo in range(NCH):
        k = mo % 2
        wsb = wos[k]
        p.dma("gpsimd", wsb[:, 0:1024], aps["wo"][mo], writes=(wsb.name,))
        for nt in range(NT):
            s = slice(nt * 512, (nt + 1) * 512)
            pg = ctx.pbig[2 + nt % 2]
            for kc in range(NCH):
                MM(ctx, pg, wsb.name, ao[kc].name,
                   w_sl=(slice(None), slice(kc * 128, (kc + 1) * 128)),
                   x_sl=(slice(None), s),
                   start=(kc == 0), stop=(kc == NCH - 1))
            x = ctx.t(f"{xpfx}{mo}")
            p.emit("vector",
                   lambda e, x=x, s=s, pg=pg, mo=mo: e.scalar_tensor_tensor(
                       x[:, s], ctx.t(pg)[:, :], bo[:, mo:mo + 1], x[:, s],
                       ALU.add, ALU.add),
                   reads=(pg, f"{xpfx}{mo}", "l2bo"), writes=(f"{xpfx}{mo}",))


def host_prep_l2_attn(inputs):
    g = lambda k: np.ascontiguousarray(np.asarray(inputs[k]))
    b = np.stack([g("L2_attn_bq").reshape(8, 128).T,
                  g("L2_attn_bk").reshape(8, 128).T], axis=1)
    bv = g("L2_attn_bv")
    bvb = np.broadcast_to(bv[None, :], (128, 1024)).astype(np.float32)
    id64 = np.vstack([np.eye(64), np.eye(64)]).astype(BFNP)
    cmask = np.zeros((128, 4 * 512), dtype=BFNP)
    for jt in range(4):
        jj = np.arange(128)[:, None] + jt * 128
        ii = np.arange(512)[None, :]
        cmask[:, jt * 512:(jt + 1) * 512] = (jj <= ii).astype(BFNP)
    return {
        "wq": host_tile_lhsT(np.ascontiguousarray(g("L2_attn_Wq").T)),
        "wk": host_tile_lhsT(np.ascontiguousarray(g("L2_attn_Wk").T)),
        "wv": host_tile_lhsT(np.ascontiguousarray(g("L2_attn_Wv").T)),
        "wo": host_tile_lhsT(np.ascontiguousarray(g("L2_attn_Wo").T)),
        "b": np.ascontiguousarray(b),
        "bvb": bvb,
        "bo": np.ascontiguousarray(g("L2_attn_bo").reshape(8, 128).T),
        "id64": id64,
        "cmask": cmask,
    }


def host_l2_pmask(c):
    bA, bB = core_blocks(c)
    m = np.zeros((2, 16, 64), dtype=np.float32)
    m[0, :bA, :] = 1.0
    m[1, :bB, :] = 1.0
    return np.ascontiguousarray(m.transpose(2, 0, 1))


def host_l2_states_all(stlocs):
    out = np.zeros((2, 16, 64, 16 * 65), dtype=np.float32)
    for c in range(NCORES):
        sl = stlocs[c].reshape(64, 4, 16, 65)
        sl = sl.reshape(64, 2, 2, 16, 65)
        bA, bB = core_blocks(c)
        for si, blk in ((0, bA), (1, bB)):
            for b in range(2):
                out[b, :, :, :] = out[b, :, :, :]
                out[b, :, :, blk * 65:(blk + 1) * 65] = \
                    sl[:, si, b, :, :].transpose(1, 0, 2)
    return np.ascontiguousarray(out)


def build_l2_attn_pre_launch():
    p = Prog()
    x_ap = p.dram_in("x", [NCH, 128, TLOC], F32)
    aps = {
        "wq": p.dram_in("wq", [8, 128, 8, 128], F32),
        "wk": p.dram_in("wk", [8, 128, 8, 128], F32),
        "wv": p.dram_in("wv", [8, 128, 8, 128], F32),
        "b": p.dram_in("b", [128, 2, 8], F32),
        "bvb": p.dram_in("bvb", [128, 1024], F32),
        "id64": p.dram_in("id64", [128, 64], BF16),
    }
    q_out = p.dram_out("qphi", [NCH, 128, TLOC], BF16)
    k_out = p.dram_out("kphi", [NCH, 128, TLOC], BF16)
    v_out = p.dram_out("vtok", [16, 128, 16 * 65], BF16)
    st_out = p.dram_out("st", [64, 4 * 16 * 65], F32)
    for c in range(NCH):
        p.sbuf(f"x{c}", [128, TLOC], F32)
    ctx = Ctx(p, n_hb=8, n_fa=6, n_ws=2, with_xn=False)
    p.sbuf("l2bvb", [128, 1024], F32)
    p.dma("sync", p._tensors["l2bvb"][:, :], aps["bvb"], writes=("l2bvb",))
    for c in range(NCH):
        p.dma("gpsimd", ctx.t(f"x{c}")[:, :], x_ap[c], writes=(f"x{c}",))
    st_dma = emit_l2_attn_pre(ctx, aps, q_out, k_out, v_out, st_out)
    return p.build(output_dma_ops=[st_dma, ctx.last_q, ctx.last_k, ctx.last_v])


def build_l2_attn_post_launch():
    p = Prog()
    x_ap = p.dram_in("x", [NCH, 128, TLOC], F32)
    aps = {
        "q": p.dram_in("qphi_in", [NCH, 128, TLOC], BF16),
        "k": p.dram_in("kphi_in", [NCH, 128, TLOC], BF16),
        "v": p.dram_in("vtok_in", [16, 128, 16 * 65], BF16),
        "st": p.dram_in("stall", [2, 16, 64, 16 * 65], F32),
        "pmask": p.dram_in("pmask", [64, 2, 16], F32),
        "cmask": p.dram_in("cmask", [128, 4 * 512], BF16),
        "wo": p.dram_in("wo", [8, 128, 8, 128], F32),
        "bo": p.dram_in("bo", [128, 8], F32),
    }
    xo_ap = p.dram_out("xo", [NCH, 128, TLOC], F32)
    for c in range(NCH):
        p.sbuf(f"x{c}", [128, TLOC], F32)
    ctx = Ctx(p, n_hb=0, n_fa=1, n_ws=0, n_rows=1, with_xn=False)
    for c in range(NCH):
        p.dma("gpsimd", ctx.t(f"x{c}")[:, :], x_ap[c], writes=(f"x{c}",))
    emit_l2_attn_post(ctx, aps)
    outs = []
    for c in range(NCH):
        outs.append(p.dma("gpsimd", xo_ap[c], ctx.t(f"x{c}")[:, :],
                          reads=(f"x{c}",), writes=(f"xo{c}",)))
    return p.build(output_dma_ops=outs)


def emit_l0_attn_pre(ctx, aps, q_out, k_out, v_out, km_out, xpfx="x",
                     v_aug=False):
    p = ctx.prog
    xn_t = ctx.hb[0:8]
    xn_names = [t.name for t in xn_t]
    stg = p.sbuf("l0stg", [128, 512], BF16)
    vstg = ([p.sbuf(f"l0vstg{t}", [128, 16 * 65], BF16) for t in range(4)]
            if v_aug else None)
    kml = p.sbuf("l0kml", [1, 32], F32)
    krow = p.sbuf("l0krow", [1, 512], F32)
    kv1 = p.sbuf("l0kv1", [1, 1], F32)
    p.emit("vector", lambda e: e.memset(kml[:, :], 0.0), writes=("l0kml",))
    for nt in range(NT):
        s = slice(nt * 512, (nt + 1) * 512)
        b = nt % 2
        emit_ln_tile(ctx, xpfx, nt, xn_names)
        for which, base, out_ap in (("q", 0, q_out), ("k", 8, k_out)):
            for m in range(NCH):
                k = m % 2
                wsb = ctx.ws[k]
                p.dma("gpsimd", wsb[:, 0:1024], aps["wqkv"][base + m],
                      writes=(wsb.name,))
                pg = ctx.pbig[k]
                for kc in range(NCH):
                    MM(ctx, pg, wsb.name, xn_names[kc],
                       w_sl=(slice(None), slice(kc * 128, (kc + 1) * 128)),
                       start=(kc == 0), stop=(kc == NCH - 1))
                ACT_(ctx, "Identity", stg.name, pg, bias="l0b",
                     bias_sl=(slice(None), slice(base + m, base + m + 1)))
                ctx.last_qk = p.dma("gpsimd", out_ap[m, :, s], stg[:, :],
                                    reads=(stg.name,), writes=(which + "o",))
                if which == "k":
                    for hh in range(2):
                        h = 2 * m + hh
                        hs = slice(hh * 64, (hh + 1) * 64)
                        fa = ctx.fa[2]
                        p.emit("scalar",
                               lambda e, hs=hs, fa=fa: e.activation(
                                   out=fa[0:64, :], in_=stg[hs, :],
                                   func=AF.Square),
                               reads=(stg.name,), writes=(fa.name,))
                        p.emit("tensor",
                               lambda e, fa=fa: e.matmul(
                                   ctx.t(ctx.prow0)[:, :], ctx.ones_col_f[0:64, :],
                                   fa[0:64, :], start=True, stop=True),
                               reads=(fa.name, "ones_col_f", ctx.prow0),
                               writes=(ctx.prow0,))
                        p.emit("scalar",
                               lambda e: e.activation(out=krow[:, :],
                                                      in_=ctx.t(ctx.prow0)[:, :],
                                                      func=AF.Copy),
                               reads=(ctx.prow0,), writes=("l0krow",))
                        p.emit("vector",
                               lambda e: e.reduce_max(kv1[:, :], krow[:, :],
                                                      axis=mybir.AxisListType.X),
                               reads=("l0krow",), writes=("l0kv1",))
                        idx = b * 16 + h
                        p.emit("vector",
                               lambda e, idx=idx: e.tensor_tensor(
                                   kml[:, idx:idx + 1], kml[:, idx:idx + 1],
                                   kv1[:, :], ALU.max),
                               reads=("l0kml", "l0kv1"), writes=("l0kml",))
        if v_aug:
            for tt in range(4):
                p.emit("vector", lambda e, tt=tt: e.memset(vstg[tt][:, :], 1.0),
                       writes=(vstg[tt].name,))
        for m in range(NCH):
            k = m % 2
            wsb = ctx.ws[k]
            p.dma("gpsimd", wsb[:, 0:1024], aps["wqkv"][16 + m], writes=(wsb.name,))
            for tt in range(4):
                pg = ctx.pbig[2 + tt % 2]
                for kc in range(NCH):
                    MM(ctx, pg, xn_names[kc], wsb.name,
                       w_sl=(slice(None), slice(tt * 128, (tt + 1) * 128)),
                       x_sl=(slice(None), slice(kc * 128, (kc + 1) * 128)),
                       ps_sl=(slice(None), slice(0, 128)),
                       start=(kc == 0), stop=(kc == NCH - 1))
                if not v_aug:
                    p.emit("vector",
                           lambda e, pg=pg, m=m: e.scalar_tensor_tensor(
                               stg[:, 0:128], ctx.t(pg)[:, 0:128], 1.0,
                               ctx.t("l0bvb")[:, m * 128:(m + 1) * 128],
                               ALU.mult, ALU.add),
                           reads=(pg, "l0bvb"), writes=(stg.name,))
                    ctx.last_v = p.dma("gpsimd",
                                       v_out[nt * 4 + tt, :, m * 128:(m + 1) * 128],
                                       stg[:, 0:128], reads=(stg.name,),
                                       writes=("vo",))
                else:
                    for hh in range(2):
                        h = 2 * m + hh
                        p.emit("vector",
                               lambda e, pg=pg, tt=tt, hh=hh, h=h: e.scalar_tensor_tensor(
                                   vstg[tt][:, h * 65:h * 65 + 64],
                                   ctx.t(pg)[:, hh * 64:(hh + 1) * 64], 1.0,
                                   ctx.t("l0bvb")[:, h * 64:(h + 1) * 64],
                                   ALU.mult, ALU.add),
                               reads=(pg, "l0bvb"), writes=(vstg[tt].name,))
        if v_aug:
            for tt in range(4):
                ctx.last_v = p.dma("gpsimd", v_out[nt * 4 + tt],
                                   vstg[tt][:, :], reads=(vstg[tt].name,),
                                   writes=("vo",))
    return p.dma("gpsimd", km_out, kml[:, :], reads=("l0kml",), writes=("kmo",))


def emit_l0_attn_post(ctx, aps, xpfx="x", dyn=False):
    p = ctx.prog
    qf = [p.sbuf(f"l0qf{c}", [128, TLOC], BF16) for c in range(NCH)]
    ao = qf
    bstore = p.sbuf("l0bst", [128, 67 * 128], BF16)
    kma = p.sbuf("l0kma", [1, NCORES * 32], F32)
    kml = p.sbuf("l0kml2", [1, 32], F32)
    bo = p.sbuf("l0bo", [128, 8], F32)
    qa2 = [p.sbuf(f"l0qa{s}", [65, 512], BF16) for s in range(2)]
    rM2 = [p.sbuf(f"l0rM{s}", [1, 512], F32) for s in range(2)]
    kt2 = [[p.sbuf(f"l0kt{s}_{k}", [65, 512], BF16) for k in range(2)]
           for s in range(2)]
    vt2 = [[p.sbuf(f"l0vt{s}_{k}", [128, 4 * 65], BF16) for k in range(2)]
           for s in range(2)]
    prna4 = [[[p.sbuf(f"l0prna{s}{j}{h}", [128, 512], BF16) for h in range(2)]
              for j in range(2)] for s in range(2)]
    prb4 = [[[p.sbuf(f"l0prb{s}{j}{h}", [128, 512], BF16) for h in range(2)]
             for j in range(2)] for s in range(2)]
    nsb2 = [p.sbuf(f"l0nsb{s}", [64, 512], F32) for s in range(2)]
    rden2 = [p.sbuf(f"l0rden{s}", [1, 512], F32) for s in range(2)]
    sq2 = [p.sbuf(f"l0sq{s}", [64, 512], F32) for s in range(2)]
    wos = [p.sbuf(f"l0wos{k}", [128, 1024], BF16) for k in range(2)]
    psc4 = [[p.psum(f"psc{s}{j}", [128, 512], F32) for j in range(2)]
            for s in range(2)]
    ppv2 = [p.psum(f"ppv{s}", [65, 512], F32) for s in range(2)]
    p.dma("gpsimd", bstore[:, :], aps["bst"], writes=("l0bst",))
    p.dma("sync", kma[:, :], aps["km"], writes=("l0kma",))
    p.emit("vector", lambda e: e.tensor_copy(kml[:, :], kma[:, 0:32]),
           reads=("l0kma",), writes=("l0kml2",))
    for i in range(1, NCORES):
        p.emit("vector", lambda e, i=i: e.tensor_tensor(
            kml[:, :], kml[:, :], kma[:, i * 32:(i + 1) * 32], ALU.max),
               reads=("l0kma", "l0kml2"), writes=("l0kml2",))
    ACT_(ctx, "Sqrt", "l0kml2", "l0kml2")
    p.dma("sync", bo[:, :], aps["bo"], writes=("l0bo",))
    for c in range(NCH):
        p.dma("gpsimd", qf[c][:, :], aps["q"][c], writes=(qf[c].name,))
    NU = {0: 16, 1: 8}
    for b in range(2):
        for slot in range(2):
            g = slice((2 * slot + b) * 512, (2 * slot + b + 1) * 512)
            for hp in range(8):
                c = hp
                heads = (2 * hp, 2 * hp + 1)
                hss = (slice(0, 64), slice(64, 128))
                for st in range(2):
                    hs = hss[st]
                    p.emit("scalar",
                           lambda e, c=c, hs=hs, g=g, st=st: e.activation(
                               out=sq2[st][:, :], in_=qf[c][hs, g],
                               func=AF.Square),
                           reads=(qf[c].name,), writes=(sq2[st].name,))
                    p.emit("tensor",
                           lambda e, st=st: e.matmul(
                               ctx.t(ctx.pbig[st])[0:1, :],
                               ctx.ones_col_f[0:64, :],
                               sq2[st][:, :], start=True, stop=True),
                           reads=(sq2[st].name, "ones_col_f", ctx.pbig[st]),
                           writes=(ctx.pbig[st],))
                    ACT_(ctx, "Sqrt", rM2[st].name, ctx.pbig[st],
                         in_sl=(slice(0, 1), slice(None)))
                    kidx = b * 16 + heads[st]
                    p.emit("vector",
                           lambda e, kidx=kidx, st=st: e.tensor_scalar(
                               rM2[st][:, :], rM2[st][:, :],
                               kml[:, kidx:kidx + 1], -1.0,
                               ALU.mult, ALU.mult),
                           reads=(rM2[st].name, "l0kml2"), writes=(rM2[st].name,))
                    p.emit("vector",
                           lambda e, c=c, hs=hs, g=g, st=st: e.tensor_copy(
                               qa2[st][0:64, :], qf[c][hs, g]),
                           reads=(qf[c].name,), writes=(qa2[st].name,))
                    p.emit("vector",
                           lambda e, st=st: e.tensor_copy(qa2[st][64:65, :],
                                                          rM2[st][:, :]),
                           reads=(rM2[st].name,), writes=(qa2[st].name,))
                for u in range(NU[slot]):
                    k = u % 2
                    for st in range(2):
                        h = heads[st]
                        if dyn:
                            p.dma("sync", kt2[st][k][:, :],
                                  aps["kreA" if slot == 0 else "kreB"][u, b, h],
                                  reads=("krelA" if slot == 0 else "krelB",),
                                  writes=(kt2[st][k].name,))
                            p.dma("sync", vt2[st][k][:, :],
                                  aps["vreA" if slot == 0 else "vreB"][u, b, h],
                                  reads=("vrelA" if slot == 0 else "vrelB",),
                                  writes=(vt2[st][k].name,))
                        else:
                            p.dma("sync", kt2[st][k][:, :],
                                  aps["kre"][slot, u, b, h],
                                  writes=(kt2[st][k].name,))
                            p.dma("sync", vt2[st][k][:, :],
                                  aps["vre"][slot, u, b, h],
                                  writes=(vt2[st][k].name,))
                    for half in range(2):
                        for st in range(2):
                            for jj in range(2):
                                jt = 2 * half + jj
                                p.emit("tensor",
                                       lambda e, k=k, jt=jt, jj=jj, st=st: e.matmul(
                                           psc4[st][jj][:, :],
                                           kt2[st][k][:, jt * 128:(jt + 1) * 128],
                                           qa2[st][:, :], start=True, stop=True),
                                       reads=(kt2[st][k].name, qa2[st].name,
                                              psc4[st][jj].name),
                                       writes=(psc4[st][jj].name,))
                        for st in range(2):
                            for jj in range(2):
                                ACT_(ctx, "Exp", prna4[st][jj][half].name,
                                     psc4[st][jj].name, scale=0.125)
                        for st in range(2):
                            for jj in range(2):
                                jt = 2 * half + jj
                                bsl = ctx.t("l0bst").ap()[:, (4 * u + jt) * 128:
                                                          (4 * u + jt + 4) * 128]
                                bsl = bsl.rearrange("p (o i) -> p o i",
                                                    i=128)[:, ::-1, :]
                                p.emit("vector",
                                       lambda e, jj=jj, bsl=bsl, st=st,
                                       half=half: e.tensor_tensor(
                                           prb4[st][jj][half][:, :].rearrange(
                                               "p (o i) -> p o i", i=128),
                                           prna4[st][jj][half][:, :].rearrange(
                                               "p (o i) -> p o i", i=128),
                                           bsl, ALU.mult),
                                       reads=(prna4[st][jj][half].name, "l0bst"),
                                       writes=(prb4[st][jj][half].name,))
                        for st in range(2):
                            for jj in range(2):
                                jt = 2 * half + jj
                                p.emit("tensor",
                                       lambda e, k=k, jt=jt, jj=jj, u=u,
                                       slot=slot, st=st, half=half: e.matmul(
                                           ppv2[st][:, :],
                                           vt2[st][k][:, jt * 65:(jt + 1) * 65],
                                           prb4[st][jj][half][:, :],
                                           start=(u == 0 and jt == 0),
                                           stop=(u == NU[slot] - 1 and jt == 3)),
                                       reads=(vt2[st][k].name,
                                              prb4[st][jj][half].name,
                                              ppv2[st].name),
                                       writes=(ppv2[st].name,))
                for st in range(2):
                    hs = hss[st]
                    p.emit("vector",
                           lambda e, st=st: e.tensor_copy(rden2[st][:, :],
                                                          ppv2[st][64:65, :]),
                           reads=(ppv2[st].name,), writes=(rden2[st].name,))
                    p.emit("vector",
                           lambda e, st=st: e.reciprocal(rden2[st][:, :],
                                                         rden2[st][:, :]),
                           reads=(rden2[st].name,), writes=(rden2[st].name,))
                    p.emit("tensor",
                           lambda e, st=st: e.matmul(
                               ctx.t(ctx.pbig[st])[0:64, :],
                               ctx.ones_row_f[:, 0:64], rden2[st][:, :],
                               start=True, stop=True),
                           reads=(rden2[st].name, "ones_row_f", ctx.pbig[st]),
                           writes=(ctx.pbig[st],))
                    ACT_(ctx, "Copy", nsb2[st].name, ppv2[st].name,
                         in_sl=(slice(0, 64), slice(None)))
                    p.emit("vector",
                           lambda e, c=c, hs=hs, g=g, st=st: e.tensor_tensor(
                               ao[c][hs, g], nsb2[st][:, :],
                               ctx.t(ctx.pbig[st])[0:64, :], ALU.mult),
                           reads=(nsb2[st].name, ctx.pbig[st]),
                           writes=(ao[c].name,))
    for mo in range(NCH):
        k = mo % 2
        wsb = wos[k]
        p.dma("gpsimd", wsb[:, 0:1024], aps["wp"][mo], writes=(wsb.name,))
        for nt in range(NT):
            s = slice(nt * 512, (nt + 1) * 512)
            pg = ctx.pbig[nt % 2]
            for kc in range(NCH):
                MM(ctx, pg, wsb.name, ao[kc].name,
                   w_sl=(slice(None), slice(kc * 128, (kc + 1) * 128)),
                   x_sl=(slice(None), s),
                   start=(kc == 0), stop=(kc == NCH - 1))
            x = ctx.t(f"{xpfx}{mo}")
            p.emit("vector",
                   lambda e, x=x, s=s, pg=pg, mo=mo: e.scalar_tensor_tensor(
                       x[:, s], ctx.t(pg)[:, :], bo[:, mo:mo + 1], x[:, s],
                       ALU.add, ALU.add),
                   reads=(pg, f"{xpfx}{mo}", "l0bo"), writes=(f"{xpfx}{mo}",))


def host_prep_l0_attn(inputs):
    g = lambda k: np.ascontiguousarray(np.asarray(inputs[k]))
    b_attn = g("L0_attn_b_attn")
    bqk = b_attn[0:2048].reshape(16, 128).T
    bvb = np.broadcast_to(b_attn[2048:3072][None, :], (128, 1024)).astype(np.float32)
    bst = np.zeros((128, 67 * 128), dtype=np.float64)
    jj = np.arange(128)[:, None]
    ii = np.arange(128)[None, :]
    for t in range(67):
        delta = 128 * (t - 3) + jj - ii
        with np.errstate(divide="ignore"):
            v = np.where(delta > 0, 1.0 / np.maximum(delta + 1.0, 1.0), np.where(delta == 0, 1.0, 0.0))
        bst[:, t * 128:(t + 1) * 128] = v
    return {
        "wqkv": host_tile_lhsT(g("L0_attn_w_attn")),
        "b": np.ascontiguousarray(bqk),
        "bvb": bvb,
        "wp": host_tile_lhsT(g("L0_attn_w_proj")),
        "bo": np.ascontiguousarray(g("L0_attn_b_proj").reshape(8, 128).T),
        "bst": bst.astype(BFNP),
    }


def host_l0_build_rel(kouts, vouts, core):
    Kg = np.zeros((2, 16, 64, SEQ), dtype=BFNP)
    Vg = np.zeros((2, 16, SEQ, 64), dtype=BFNP)
    for c in range(NCORES):
        kc_ = kouts[c].reshape(NCH, 128, TLOC)
        vc_ = vouts[c].reshape(16, 128, 1024)
        bA, bB = core_blocks(c)
        for gi, (si, b) in enumerate(((0, 0), (0, 1), (1, 0), (1, 1))):
            blk = bA if si == 0 else bB
            pos0 = blk * BLKSZ
            colr = slice(gi * 512, (gi + 1) * 512)
            for h in range(16):
                ch, hh = h // 2, h % 2
                Kg[b, h, :, pos0:pos0 + 512] = kc_[ch, hh * 64:(hh + 1) * 64, colr]
            for jt in range(4):
                tt = gi * 4 + jt
                for h in range(16):
                    Vg[b, h, pos0 + jt * 128:pos0 + (jt + 1) * 128, :] = \
                        vc_[tt, :, h * 64:(h + 1) * 64]
    Kre = np.zeros((2, 16, 2, 16, 65, 512), dtype=BFNP)
    Vre = np.zeros((2, 16, 2, 16, 128, 4, 65), dtype=BFNP)
    bA, bB = core_blocks(core)
    for si, g0 in ((0, bA), (1, bB)):
        for u in range(16):
            blk = g0 + u
            if blk >= NBLK:
                continue
            ps = slice(blk * BLKSZ, (blk + 1) * BLKSZ)
            for b in range(2):
                Kre[si, u, b, :, 0:64, :] = Kg[b, :, :, ps]
                Kre[si, u, b, :, 64, :] = 1.0
                vv = Vg[b, :, ps, :].reshape(16, 4, 128, 64)
                Vre[si, u, b, :, :, :, 0:64] = vv.transpose(0, 2, 1, 3)
                Vre[si, u, b, :, :, :, 64] = 1.0
    return Kre, Vre


def build_l0_attn_pre_launch():
    p = Prog()
    x_ap = p.dram_in("x", [NCH, 128, TLOC], F32)
    aps = {
        "wqkv": p.dram_in("wqkv", [24, 128, 8, 128], F32),
        "b": p.dram_in("b", [128, 16], F32),
        "bvb": p.dram_in("bvb", [128, 1024], F32),
    }
    q_out = p.dram_out("q", [NCH, 128, TLOC], BF16)
    k_out = p.dram_out("k", [NCH, 128, TLOC], BF16)
    v_out = p.dram_out("v", [16, 128, 1024], BF16)
    km_out = p.dram_out("km", [1, 32], F32)
    for c in range(NCH):
        p.sbuf(f"x{c}", [128, TLOC], F32)
    ctx = Ctx(p, n_hb=8, n_fa=4, n_ws=2, with_xn=False)
    p.sbuf("l0b", [128, 16], F32)
    p.sbuf("l0bvb", [128, 1024], F32)
    p.dma("sync", p._tensors["l0b"][:, :], aps["b"], writes=("l0b",))
    p.dma("sync", p._tensors["l0bvb"][:, :], aps["bvb"], writes=("l0bvb",))
    for c in range(NCH):
        p.dma("gpsimd", ctx.t(f"x{c}")[:, :], x_ap[c], writes=(f"x{c}",))
    km_dma = emit_l0_attn_pre(ctx, aps, q_out, k_out, v_out, km_out)
    return p.build(output_dma_ops=[km_dma, ctx.last_qk, ctx.last_v])


def build_l0_attn_post_launch():
    p = Prog()
    x_ap = p.dram_in("x", [NCH, 128, TLOC], F32)
    aps = {
        "q": p.dram_in("qin", [NCH, 128, TLOC], BF16),
        "kre": p.dram_in("kre", [2, 16, 2, 16, 65, 512], BF16),
        "vre": p.dram_in("vre", [2, 16, 2, 16, 128, 4 * 65], BF16),
        "km": p.dram_in("km", [NCORES, 1, 32], F32),
        "bst": p.dram_in("bst", [128, 67 * 128], BF16),
        "wp": p.dram_in("wp", [8, 128, 8, 128], F32),
        "bo": p.dram_in("bo", [128, 8], F32),
    }
    xo_ap = p.dram_out("xo", [NCH, 128, TLOC], F32)
    for c in range(NCH):
        p.sbuf(f"x{c}", [128, TLOC], F32)
    ctx = Ctx(p, n_hb=0, n_fa=1, n_ws=0, n_rows=1, with_xn=False,
              n_pbig=2, with_prow=False)
    for c in range(NCH):
        p.dma("gpsimd", ctx.t(f"x{c}")[:, :], x_ap[c], writes=(f"x{c}",))
    emit_l0_attn_post(ctx, aps)
    outs = []
    for c in range(NCH):
        outs.append(p.dma("gpsimd", xo_ap[c], ctx.t(f"x{c}")[:, :],
                          reads=(f"x{c}",), writes=(f"xo{c}",)))
    return p.build(output_dma_ops=outs)


_BUILT = {}


def _get(name):
    if name not in _BUILT:
        _BUILT[name] = globals()[f"build_{name}_launch"]()
    return _BUILT[name]


def kernel_unfused(**inputs):
    import ml_dtypes as mld
    bf = mld.bfloat16
    x = np.asarray(inputs["L0_attn_hidden_states"], dtype=np.float32)
    shards = [s.reshape(NCH, 128, TLOC) for s in host_shard_x(x)]

    def run(name, per_core):
        nc = _get(name)
        res = run_bass_kernel_spmd(nc, per_core, core_ids=list(range(NCORES)))
        return res.results

    w = host_prep_l0_attn(inputs)
    r = run("l0_attn_pre", [dict(x=shards[c], wqkv=w["wqkv"], b=w["b"],
                                 bvb=w["bvb"]) for c in range(NCORES)])
    kouts = [r[c]["k"].view(bf) for c in range(NCORES)]
    vouts = [r[c]["v"].view(bf) for c in range(NCORES)]
    kmall = np.stack([r[c]["km"].reshape(1, 32) for c in range(NCORES)])
    qins = [r[c]["q"].view(bf).reshape(NCH, 128, TLOC) for c in range(NCORES)]
    in2 = []
    for c in range(NCORES):
        Kre, Vre = host_l0_build_rel(kouts, vouts, c)
        in2.append(dict(x=shards[c], qin=qins[c], kre=Kre,
                        vre=Vre.reshape(2, 16, 2, 16, 128, 4 * 65),
                        km=kmall, bst=w["bst"], wp=w["wp"], bo=w["bo"]))
    r = run("l0_attn_post", in2)
    shards = [np.ascontiguousarray(r[c]["xo"].reshape(NCH, 128, TLOC))
              for c in range(NCORES)]

    w = host_prep_l0_ffn(inputs)
    r = run("l0_ffn", [dict(x=shards[c], **w) for c in range(NCORES)])
    shards = [np.ascontiguousarray(r[c]["xo"].reshape(NCH, 128, TLOC))
              for c in range(NCORES)]

    w = host_prep_l1_attn(inputs)
    r = run("l1_attn", [dict(x=shards[c], **w) for c in range(NCORES)])
    shards = [np.ascontiguousarray(r[c]["xo"].reshape(NCH, 128, TLOC))
              for c in range(NCORES)]

    w = host_prep_l1_ffn(inputs)
    r = run("l1_ffn", [dict(x=shards[c], **w) for c in range(NCORES)])
    shards = [np.ascontiguousarray(r[c]["xo"].reshape(NCH, 128, TLOC))
              for c in range(NCORES)]

    w = host_prep_l2_attn(inputs)
    r = run("l2_attn_pre", [dict(x=shards[c], wq=w["wq"], wk=w["wk"],
                                 wv=w["wv"], b=w["b"], bvb=w["bvb"],
                                 id64=w["id64"]) for c in range(NCORES)])
    stlocs = [r[c]["st"].reshape(64, 4 * 16 * 65) for c in range(NCORES)]
    stall = host_l2_states_all(stlocs)
    in2 = [dict(x=shards[c],
                qphi_in=r[c]["qphi"].view(bf).reshape(NCH, 128, TLOC),
                kphi_in=r[c]["kphi"].view(bf).reshape(NCH, 128, TLOC),
                vtok_in=r[c]["vtok"].view(bf).reshape(16, 128, 16 * 65),
                stall=stall, pmask=host_l2_pmask(c), cmask=w["cmask"],
                wo=w["wo"], bo=w["bo"]) for c in range(NCORES)]
    r = run("l2_attn_post", in2)
    shards = [np.ascontiguousarray(r[c]["xo"].reshape(NCH, 128, TLOC))
              for c in range(NCORES)]

    w = host_prep_l2_ffn(inputs)
    r = run("l2_ffn", [dict(x=shards[c], **w) for c in range(NCORES)])
    shards = [np.ascontiguousarray(r[c]["xo"].reshape(NCH, 128, TLOC))
              for c in range(NCORES)]

    w = host_prep_l3_attn(inputs)
    r = run("l3_attn_pre", [dict(x=shards[c], wqkv=w["wqkv"], rf=w["rf"])
                            for c in range(NCORES)])
    kvall = np.stack([r[c]["kv"].reshape(64, 2 * 16 * 65)
                      for c in range(NCORES)])
    in2 = [dict(x=shards[c],
                pqin=r[c]["pq"].view(bf).reshape(NCH, 128, TLOC),
                kvall=kvall, wout=w["wout"], bo=w["bo"])
           for c in range(NCORES)]
    r = run("l3_attn_post", in2)
    shards = [np.ascontiguousarray(r[c]["xo"].reshape(NCH, 128, TLOC))
              for c in range(NCORES)]

    w = host_prep_l3_ffn(inputs)
    r = run("l3_ffn", [dict(x=shards[c], **w) for c in range(NCORES)])
    shards = [r[c]["xo"].reshape(NCH, 128, TLOC) for c in range(NCORES)]

    return host_unshard_x([s.reshape(D, TLOC) for s in shards])


def host_l0_tbl(c):
    bA, bB = core_blocks(c)
    return np.array([[bA, bB]], dtype=np.int32)


def host_l2_pmask2(c):
    m = np.zeros((2, 8, 2), dtype=np.float32)
    bA, bB = core_blocks(c)
    for slot, myb in ((0, bA), (1, bB)):
        for r in range(8):
            for si in range(2):
                blk = r if si == 0 else 15 - r
                if blk < myb:
                    m[slot, r, si] = 1.0
    flat = m.reshape(1, 32)
    return np.ascontiguousarray(np.broadcast_to(flat, (64, 32)).copy())


def build_fused(dbg=False, upto=None):
    p = Prog()
    x_ap = p.dram_in("x", [NCH, 128, TLOC], F32)
    xo_ap = p.dram_out("xo", [NCH, 128, TLOC], F32)
    W = {}
    W["a0_wqkv"] = p.dram_in("a0_wqkv", [24, 128, 8, 128], F32)
    W["a0_b"] = p.dram_in("a0_b", [128, 16], F32)
    W["a0_bvb"] = p.dram_in("a0_bvb", [128, 1024], F32)
    W["a0_bst"] = p.dram_in("a0_bst", [128, 67 * 128], BF16)
    W["a0_wp"] = p.dram_in("a0_wp", [8, 128, 8, 128], F32)
    W["a0_bo"] = p.dram_in("a0_bo", [128, 8], F32)
    W["a0_tbl"] = p.dram_in("a0_tbl", [1, 2], I32)
    W["f0_wg"] = p.dram_in("f0_wg", [24, 128, 8, 128], I32)
    W["f0_wu"] = p.dram_in("f0_wu", [24, 128, 8, 128], I32)
    W["f0_wd"] = p.dram_in("f0_wd", [8, 128, 24, 128], I32)
    for k_ in ("wq", "wk", "wv"):
        W[f"a1_{k_}"] = p.dram_in(f"a1_{k_}", [8, 128, 8, 128], F32)
    W["a1_b"] = p.dram_in("a1_b", [128, 3, 8], F32)
    W["a1_rt"] = p.dram_in("a1_rt", [128, 64], BF16)
    W["f1_bw"] = p.dram_in("f1_bw", [8, 128, 8, 128], F32)
    W["f1_sat"] = p.dram_in("f1_sat", [128, 8, 16], F32)
    W["f1_sbt"] = p.dram_in("f1_sbt", [16, 1024], F32)
    W["f1_rat"] = p.dram_in("f1_rat", [128, 6, 8, 16], F32)
    W["f1_rbt"] = p.dram_in("f1_rbt", [16, 6 * 1024], F32)
    W["f1_rwt"] = p.dram_in("f1_rwt", [128, 8, 6], F32)
    W["f1_bb"] = p.dram_in("f1_bb", [128, 8], F32)
    W["f1_rb"] = p.dram_in("f1_rb", [1, 6], F32)
    W["sel6"] = p.dram_in("sel6", [6, 6 * 16], BF16)
    W["ident"] = p.dram_in("ident", [128, 128], BF16)
    W["sel8"] = p.dram_in("sel8", [8, 8 * 128], BF16)
    W["id64"] = p.dram_in("id64", [128, 64], BF16)
    for k_ in ("wq", "wk", "wv", "wo"):
        W[f"a2_{k_}"] = p.dram_in(f"a2_{k_}", [8, 128, 8, 128], F32)
    W["a2_b"] = p.dram_in("a2_b", [128, 2, 8], F32)
    W["a2_bvb"] = p.dram_in("a2_bvb", [128, 1024], F32)
    W["a2_bo"] = p.dram_in("a2_bo", [128, 8], F32)
    W["a2_cmask"] = p.dram_in("a2_cmask", [128, 4 * 512], BF16)
    W["a2_pmask2"] = p.dram_in("a2_pmask2", [64, 32], F32)
    W["f2_gw"] = p.dram_in("f2_gw", [128, 8, 8], F32)
    W["f2_w1"] = p.dram_in("f2_w1", [8, 16, 128, 8, 128], F32)
    W["f2_w2"] = p.dram_in("f2_w2", [8, 8, 128, 16, 128], F32)
    W["f2_b1"] = p.dram_in("f2_b1", [128, 8, 16], F32)
    W["f2_b2"] = p.dram_in("f2_b2", [128, 8, 8], F32)
    W["f2_lng"] = p.dram_in("f2_lng", [128, 8, 8], F32)
    W["f2_lnb"] = p.dram_in("f2_lnb", [128, 8, 8], F32)
    W["a3_wqkv"] = p.dram_in("a3_wqkv", [24, 128, 8, 128], F32)
    W["a3_rf"] = p.dram_in("a3_rf", [128, 16 * 64], F32)
    W["a3_wout"] = p.dram_in("a3_wout", [8, 128, 8, 128], F32)
    W["a3_bo"] = p.dram_in("a3_bo", [128, 8], F32)
    W["f3_gw"] = p.dram_in("f3_gw", [128, 8, 8], F32)
    W["f3_w1"] = p.dram_in("f3_w1", [8, 32, 128, 8, 128], F32)
    W["f3_w3"] = p.dram_in("f3_w3", [8, 32, 128, 8, 128], F32)
    W["f3_w2a"] = p.dram_in("f3_w2a", [8, 8, 128, 24, 128], F32)
    W["f3_w2b"] = p.dram_in("f3_w2b", [8, 8, 128, 8, 128], F32)

    nc = p.nc
    def idram(name, shape, dtype):
        h = nc.dram_tensor(name, list(shape), dtype)
        p._tensors[name] = h
        return h
    l0q = idram("l0q_i", [NCH, 128, TLOC], BF16)
    l0k = idram("l0k_i", [NCH * 128, TLOC], BF16)
    l0v = idram("l0v_i", [16 * 128, 16 * 65], BF16)
    l0km = idram("l0km_i", [1, 32], F32)
    l0k_ag = idram("l0k_ag", [8 * NCH * 128, TLOC], BF16)
    l0v_ag = idram("l0v_ag", [8 * 16 * 128, 16 * 65], BF16)
    l0km_ag = idram("l0km_ag", [8, 32], F32)
    kcan = idram("kcan", [23, 2, 16, 65, 512], BF16)
    vcan = idram("vcan", [23, 2, 16, 128, 4 * 65], BF16)
    krelA = idram("krelA", [16, 2, 16, 65, 512], BF16)
    vrelA = idram("vrelA", [16, 2, 16, 128, 4 * 65], BF16)
    krelB = idram("krelB", [8, 2, 16, 65, 512], BF16)
    vrelB = idram("vrelB", [8, 2, 16, 128, 4 * 65], BF16)
    a2q = idram("a2q_i", [NCH, 128, TLOC], BF16)
    a2k = idram("a2k_i", [NCH, 128, TLOC], BF16)
    a2v = idram("a2v_i", [16, 128, 16 * 65], BF16)
    a2st = idram("a2st_i", [64, 4 * 16 * 65], F32)
    a2st_ag = idram("a2st_ag", [8 * 64, 4 * 16 * 65], F32)
    a3pq = idram("a3pq_i", [NCH, 128, TLOC], BF16)
    a3kv = idram("a3kv_i", [64, 2 * 16 * 65], F32)
    a3kv_ag = idram("a3kv_ag", [8 * 64, 2 * 16 * 65], F32)

    RG = [list(range(NCORES))]

    def allgather(src, dst, tag, dep=None):
        p.emit("gpsimd",
               lambda e: e.collective_compute(
                   "AllGather", ALU.bypass, replica_groups=RG,
                   ins=[p._tensors[src].ap()], outs=[p._tensors[dst].ap()]),
               reads=(src,) if dep is None else (src, dep),
               writes=(dst,), tag=tag, force_inc=True)

    xnames = tuple(f"x{c}" for c in range(NCH))
    dbg_aps = {}
    if dbg:
        for i in range(1, 8):
            dbg_aps[i] = p.dram_out(f"xdbg{i}", [NCH, 128, TLOC], F32)

    def dump_dbg(i):
        if dbg:
            for c in range(NCH):
                p.dma("gpsimd", dbg_aps[i][c], p._tensors[f"x{c}"][:, :],
                      reads=(f"x{c}",), writes=(f"xdbg{i}_{c}",))
    for c in range(NCH):
        p.sbuf(f"x{c}", [128, TLOC], F32)
    p.sbuf_shared("ones_col_f", [128, 1], F32)
    p.sbuf_shared("ones_row_f", [1, 128], F32)
    p.sbuf_shared("ones_col_b", [128, 1], BF16)
    p.sbuf_shared("ones_row_b", [1, 128], BF16)
    p.sbuf_shared("sel6_sb", [6, 6 * 16], BF16)
    p.sbuf_shared("ident_sb", [128, 128], BF16)
    p.sbuf_shared("sel8_sb", [8, 8 * 128], BF16)
    p.sbuf_shared("id64_sb", [128, 64], BF16)

    with p.phase():
        ctx = Ctx(p, n_hb=8, n_fa=4, n_ws=2, with_xn=False, pfx="p0")
        p.sbuf("l0b", [128, 16], F32)
        p.sbuf("l0bvb", [128, 1024], F32)
        p.dma("sync", p._tensors["l0b"][:, :], W["a0_b"], writes=("l0b",))
        p.dma("sync", p._tensors["l0bvb"][:, :], W["a0_bvb"], writes=("l0bvb",))
        for c in range(NCH):
            p.dma(("gpsimd", "sync", "scalar")[c % 3],
                  p._tensors[f"x{c}"][:, :], x_ap[c], writes=(f"x{c}",))
        emit_l0_attn_pre(ctx, {"wqkv": W["a0_wqkv"]}, l0q.ap(),
                         l0k.ap().rearrange("(m p) t -> m p t", p=128),
                         l0v.ap().rearrange("(m p) t -> m p t", p=128),
                         l0km.ap(), v_aug=True)
        allgather("l0k_i", "l0k_ag", "agk", dep="ko")
        allgather("l0v_i", "l0v_ag", "agv", dep="vo")
        allgather("l0km_i", "l0km_ag", "agkm", dep="kmo")

    with p.phase():
        ones_sb = p.sbuf("ro_ones", [128, 512], BF16)
        zero_sb = p.sbuf("ro_zero", [128, 2080], BF16)
        p.emit("vector", lambda e: e.memset(ones_sb[:, :], 1.0),
               writes=("ro_ones",))
        p.emit("vector", lambda e: e.memset(zero_sb[:, :], 0.0),
               writes=("ro_zero",))
        kag = l0k_ag.ap().rearrange("(r m hh p) t -> r m hh p t",
                                    m=NCH, hh=2, p=64)
        vag = l0v_ag.ap().rearrange("(r tt p) f -> r tt p f", tt=16, p=128)
        QQ = ("sync", "scalar")
        _wcnt = {}
        def wal(base, qn):
            c = _wcnt.get((base, qn), 0)
            _wcnt[(base, qn)] = c + 1
            return f"{base}_{qn[:2]}{c % 4}"
        K_AL = tuple(f"kcan_{q[:2]}{j}" for q in QQ for j in range(4))
        V_AL = tuple(f"vcan_{q[:2]}{j}" for q in QQ for j in range(4))
        for g in range(NBLK):
            r, si = (g, 0) if g <= 7 else (15 - g, 1)
            qn = QQ[g % 2]
            ksrc = kag[r].rearrange("m hh p (gi cc) -> gi m hh p cc", gi=4)
            ksrc = ksrc[2 * si:2 * si + 2]
            p.dma(qn, kcan.ap()[g, :, :, 0:64, :].rearrange(
                "b (m hh) p c -> b m hh p c", hh=2), ksrc,
                reads=("l0k_ag",), writes=(wal("kcan", qn),))
            p.dma(qn, kcan.ap()[g, :, :, 64, :].rearrange(
                "b h c -> (b h) c"), ones_sb[0:32, :],
                reads=("ro_ones",), writes=(wal("kcan", qn),))
            vsrc = vag[r].rearrange("(gi jt) p (h f) -> gi jt h p f",
                                    gi=4, h=16)
            for b in range(2):
                for jt in range(4):
                    qv = QQ[(g + b + jt) % 2]
                    p.dma(qv,
                          vcan.ap()[g, b].rearrange(
                              "h p (jt f) -> h p jt f", jt=4)[:, :, jt, :],
                          vsrc[2 * si + b, jt],
                          reads=("l0v_ag",), writes=(wal("vcan", qv),))
        zk = kcan.ap().rearrange("g b h r c -> (g b h r) c")[16 * 2080:, :]
        nrow = 7 * 2080
        CH = 512
        for i in range(0, nrow, CH):
            n = min(CH, nrow - i)
            qz = QQ[(i // CH) % 2]
            p.dma(qz, zk[i:i + n, :],
                  zero_sb[:, 0:(n * 512) // 128],
                  reads=("ro_zero",), writes=(wal("kcan", qz),))
        zv = vcan.ap().rearrange("g b h p f -> (g b h p) f")[16 * 4096:, :]
        nrow = 7 * 4096
        for i in range(0, nrow, 1024):
            n = min(1024, nrow - i)
            qz = QQ[(i // 1024) % 2]
            p.dma(qz, zv[i:i + n, :],
                  zero_sb[:, 0:(n * 260) // 128],
                  reads=("ro_zero",), writes=(wal("vcan", qz),))
        tblg = p.sbuf("l0tblg", [1, 2], I32)
        p.dma("sync", tblg[:, :], W["a0_tbl"], writes=("l0tblg",))

        def _slab(e, dst, srcap, which, sz):
            rv = e.value_load(tblg[0:1, 0:1] if which == 0 else tblg[0:1, 1:2])
            return e.dma_start(out=dst.ap(), in_=srcap[ds(rv, sz)])
        p.emit("sync", lambda e: _slab(e, krelA, kcan.ap(), 0, 16),
               reads=("l0tblg",) + K_AL, writes=("krelA",), dma=True)
        p.emit("scalar", lambda e: _slab(e, vrelA, vcan.ap(), 0, 16),
               reads=("l0tblg",) + V_AL, writes=("vrelA",), dma=True)
        p.emit("sync", lambda e: _slab(e, krelB, kcan.ap(), 1, 8),
               reads=("l0tblg",) + K_AL, writes=("krelB",), dma=True)
        p.emit("scalar", lambda e: _slab(e, vrelB, vcan.ap(), 1, 8),
               reads=("l0tblg",) + V_AL, writes=("vrelB",), dma=True)

    with p.phase():
        ctx = Ctx(p, n_hb=0, n_fa=1, n_ws=0, n_rows=1, with_xn=False,
                  n_pbig=2, with_prow=False, pfx="p2")
        aps = {"q": l0q.ap(), "km": l0km_ag.ap().rearrange("r (o k) -> r o k", o=1),
               "bst": W["a0_bst"], "wp": W["a0_wp"], "bo": W["a0_bo"],
               "kreA": krelA.ap(), "kreB": krelB.ap(),
               "vreA": vrelA.ap(), "vreB": vrelB.ap()}
        emit_l0_attn_post(ctx, aps, dyn=True)
        dump_dbg(1)

    if upto == 2:
        return p.build(output_dma_ops=[])

    with p.phase():
        ctx = Ctx(p, n_hb=32, n_fa=4, n_ws=4, with_xn=False, pfx="p3")
        emit_l0_ffn(ctx, W["f0_wg"], W["f0_wu"], W["f0_wd"])
        dump_dbg(2)

    with p.phase():
        ctx = Ctx(p, n_hb=8, n_fa=2, n_ws=2, with_xn=False, pfx="p4")
        emit_l1_attn(ctx, {"wq": W["a1_wq"], "wk": W["a1_wk"],
                           "wv": W["a1_wv"], "b": W["a1_b"], "rt": W["a1_rt"]})
        dump_dbg(3)

    with p.phase():
        ctx = Ctx(p, n_hb=32, n_fa=4, n_ws=2, pfx="p5")
        emit_l1_ffn(ctx, {"bw": W["f1_bw"], "sat": W["f1_sat"],
                          "sbt": W["f1_sbt"], "rat": W["f1_rat"],
                          "rbt": W["f1_rbt"], "rwt": W["f1_rwt"],
                          "bb": W["f1_bb"], "rb": W["f1_rb"],
                          "sel6": W["sel6"], "ident": W["ident"]})
        dump_dbg(4)

    with p.phase():
        ctx = Ctx(p, n_hb=16, n_fa=6, n_ws=2, with_xn=False, pfx="p6")
        p.sbuf("l2bvb", [128, 1024], F32)
        p.dma("sync", p._tensors["l2bvb"][:, :], W["a2_bvb"], writes=("l2bvb",))
        emit_l2_attn_pre(ctx, {"wq": W["a2_wq"], "wk": W["a2_wk"],
                               "wv": W["a2_wv"], "b": W["a2_b"],
                               "id64": W["id64"]},
                         a2q.ap(), a2k.ap(), a2v.ap(), a2st.ap())
        allgather("a2st_i", "a2st_ag", "agst", dep="sto")

    with p.phase():
        ctx = Ctx(p, n_hb=0, n_fa=1, n_ws=0, n_rows=1, with_xn=False, pfx="p7")
        emit_l2_attn_post(ctx, {"q": a2q.ap(), "k": a2k.ap(), "v": a2v.ap(),
                                "stall_ag": a2st_ag.ap(),
                                "pmask2": W["a2_pmask2"],
                                "cmask": W["a2_cmask"], "wo": W["a2_wo"],
                                "bo": W["a2_bo"]}, ag=True)
        dump_dbg(5)

    with p.phase():
        ctx = Ctx(p, n_hb=48, n_fa=8, n_ws=2, pfx="p8")
        p.sbuf("l2sq", [128, 512], F32)
        p.sbuf("l2sqb", [128, 512], BF16)
        p.sbuf("l2t1", [128, 512], F32)
        emit_l2_ffn(ctx, {"gw": W["f2_gw"], "w1": W["f2_w1"], "w2": W["f2_w2"],
                          "b1": W["f2_b1"], "b2": W["f2_b2"],
                          "lng": W["f2_lng"], "lnb": W["f2_lnb"],
                          "sel8": W["sel8"], "ident": W["ident"]})
        dump_dbg(6)

    with p.phase():
        ctx = Ctx(p, n_hb=8, n_fa=2, n_ws=2, with_xn=False, pfx="p9")
        emit_l3_attn_pre(ctx, {"wqkv": W["a3_wqkv"], "rf": W["a3_rf"]},
                         a3pq.ap(), a3kv.ap())
        allgather("a3kv_i", "a3kv_ag", "agkv3", dep="kvo")

    with p.phase():
        ctx = Ctx(p, n_hb=0, n_fa=2, n_ws=2, with_xn=False, pfx="pa")
        emit_l3_attn_post(ctx, {"pq": a3pq.ap(), "kvall": a3kv_ag.ap(),
                                "kv_2d": True, "wout": W["a3_wout"],
                                "bo": W["a3_bo"]})
        dump_dbg(7)

    with p.phase():
        ctx = Ctx(p, n_hb=64, n_fa=2, n_ws=2, n_rows=4, pfx="pb")
        emit_l3_ffn(ctx, {"gw": W["f3_gw"], "w1": W["f3_w1"], "w3": W["f3_w3"],
                          "w2a": W["f3_w2a"], "w2b": W["f3_w2b"],
                          "sel8": W["sel8"], "ident": W["ident"]})
        outs = []
        for c in range(NCH):
            outs.append(p.dma("gpsimd", xo_ap[c], p._tensors[f"x{c}"][:, :],
                              reads=(f"x{c}",), writes=(f"xo{c}",)))
    return p.build(output_dma_ops=outs)


def host_prep_fused(inputs):
    w0a = host_prep_l0_attn(inputs)
    w0f = host_prep_l0_ffn(inputs)
    w1a = host_prep_l1_attn(inputs)
    w1f = host_prep_l1_ffn(inputs)
    w2a = host_prep_l2_attn(inputs)
    w2f = host_prep_l2_ffn(inputs)
    w3a = host_prep_l3_attn(inputs)
    w3f = host_prep_l3_ffn(inputs)
    m = {
        "a0_wqkv": w0a["wqkv"], "a0_b": w0a["b"], "a0_bvb": w0a["bvb"],
        "a0_bst": w0a["bst"], "a0_wp": w0a["wp"], "a0_bo": w0a["bo"],
        "f0_wg": w0f["wg"], "f0_wu": w0f["wu"], "f0_wd": w0f["wd"],
        "a1_wq": w1a["wq"], "a1_wk": w1a["wk"], "a1_wv": w1a["wv"],
        "a1_b": w1a["b"], "a1_rt": w1a["rt"],
        "f1_bw": w1f["bw"], "f1_sat": w1f["sat"], "f1_sbt": w1f["sbt"],
        "f1_rat": w1f["rat"], "f1_rbt": w1f["rbt"], "f1_rwt": w1f["rwt"],
        "f1_bb": w1f["bb"], "f1_rb": w1f["rb"],
        "sel6": w1f["sel6"], "ident": w1f["ident"],
        "a2_wq": w2a["wq"], "a2_wk": w2a["wk"], "a2_wv": w2a["wv"],
        "a2_wo": w2a["wo"], "a2_b": w2a["b"], "a2_bvb": w2a["bvb"],
        "a2_bo": w2a["bo"], "a2_cmask": w2a["cmask"], "id64": w2a["id64"],
        "f2_gw": w2f["gw"], "f2_w1": w2f["w1"], "f2_w2": w2f["w2"],
        "f2_b1": w2f["b1"], "f2_b2": w2f["b2"], "f2_lng": w2f["lng"],
        "f2_lnb": w2f["lnb"], "sel8": w2f["sel8"],
        "a3_wqkv": w3a["wqkv"], "a3_rf": w3a["rf"], "a3_wout": w3a["wout"],
        "a3_bo": w3a["bo"],
        "f3_gw": w3f["gw"], "f3_w1": w3f["w1"], "f3_w3": w3f["w3"],
        "f3_w2a": w3f["w2a"], "f3_w2b": w3f["w2b"],
    }
    return m


def kernel_fused(**inputs):
    x = np.asarray(inputs["L0_attn_hidden_states"], dtype=np.float32)
    shards = [s.reshape(NCH, 128, TLOC) for s in host_shard_x(x)]
    if "fused" not in _BUILT:
        _BUILT["fused"] = build_fused()
    nc = _BUILT["fused"]
    wmap = host_prep_fused(inputs)
    in_maps = []
    for c in range(NCORES):
        m = dict(wmap)
        m["x"] = shards[c]
        m["a0_tbl"] = host_l0_tbl(c)
        m["a2_pmask2"] = host_l2_pmask2(c)
        in_maps.append(m)
    res = run_bass_kernel_spmd(nc, in_maps, core_ids=list(range(NCORES)))
    return host_unshard_x([res.results[c]["xo"].reshape(D, TLOC)
                           for c in range(NCORES)])


kernel = kernel_fused
